# version 04
# baseline (speedup 1.0000x reference)
import os
os.environ.setdefault("GAUGE_NTFF_IGNORE_DMA", "1")
import numpy as np
import concourse.bass as bass
import concourse.mybir as mybir

F32 = mybir.dt.float32
BF16 = mybir.dt.bfloat16
AF = mybir.ActivationFunctionType
ALU = mybir.AluOpType
AX = mybir.AxisListType.X

ENGINES = ("sync", "vector", "scalar", "tensor", "gpsimd")

D = 2048
NT = D // 128
TOK = 4096
R = 2048
H = 512
FF = 8192
NL = 4
KTOT = 8192
SCALE = float(1.0 / np.sqrt(2048.0))
EPS = 1e-5


class Builder:

    def __init__(self, nc):
        self.nc = nc
        self.ops = {e: [] for e in ENGINES}
        self.cnt = {}
        self.sems = {}

    def sem(self, name):
        if name not in self.sems:
            self.sems[name] = self.nc.alloc_semaphore(name)
            self.cnt[name] = 0
        return self.sems[name]

    def emit(self, eng, method, *args, waits=(), inc=None, inc_by=1, **kwargs):
        tok = None
        incs = []
        if inc is not None:
            self.sem(inc)
            self.cnt[inc] += inc_by
            tok = (inc, self.cnt[inc])
            incs = [(inc, inc_by)]
        ws = [w for w in waits if w is not None]
        for w in ws:
            self.sem(w[0])
        self.ops[eng].append((method, args, kwargs, ws, incs))
        return tok

    def dma(self, q, out, in_, sem=None, waits=(), **kw):
        return self.emit(q, "dma_start", waits=waits, inc=sem or ("q_" + q),
                         inc_by=16, out=out, in_=in_, **kw)

    def pe(self, method, *a, **kw):
        kw.setdefault("inc", "pe")
        return self.emit("tensor", method, *a, **kw)

    def dve(self, method, *a, **kw):
        kw.setdefault("inc", "dve")
        return self.emit("vector", method, *a, **kw)

    def act(self, method, *a, **kw):
        kw.setdefault("inc", "act")
        return self.emit("scalar", method, *a, **kw)

    def pool(self, method, *a, **kw):
        kw.setdefault("inc", "pool")
        return self.emit("gpsimd", method, *a, **kw)

    def wait(self, eng, *toks):
        for t in toks:
            if t is not None:
                self.sem(t[0])
                self.ops[eng].append(("wait_ge", (), {}, [t], []))

    def phase_barrier(self):
        snap = [(s, v) for s, v in self.cnt.items() if v > 0]
        for eng in ENGINES:
            for s, v in snap:
                self.wait(eng, (s, v))

    def barrier_token(self, eng):
        sem = {"tensor": "pe", "vector": "dve", "scalar": "act", "gpsimd": "pool",
               "sync": "q_sync_nop"}[eng]
        return self.emit(eng, "nop", inc=sem)

    def replay(self, eng_name, E):
        hi = {}
        for method, args, kwargs, waits, incs in self.ops[eng_name]:
            for s, v in waits:
                if hi.get(s, -1) < v:
                    E.wait_ge(self.sems[s], v)
                    hi[s] = v
            if method == "wait_ge":
                continue
            ins = getattr(E, method)(*args, **kwargs)
            for s, n in incs:
                ins.then_inc(self.sems[s], n)

    def finalize(self):
        nc = self.nc
        with nc.Block() as block:
            block.sync(lambda E: self.replay("sync", E))
            block.vector(lambda E: self.replay("vector", E))
            block.scalar(lambda E: self.replay("scalar", E))
            block.tensor(lambda E: self.replay("tensor", E))
            block.gpsimd(lambda E: self.replay("gpsimd", E))
        return nc


class SB:

    def __init__(self, nc, limit=None):
        self.nc = nc
        self.base = (nc.sbuf_base + 63) // 64 * 64
        self.limit = limit if limit is not None else (nc.sbuf_top - self.base - 1024)
        self.blocks = []
        self.n = 0

    def alloc(self, name, shape, dtype, align=64):
        assert len(shape) == 2 and shape[0] <= 128
        bpe = 2 if dtype == BF16 else 4
        size = shape[1] * bpe
        pos = 0
        for s, e, _ in sorted(self.blocks):
            if s - pos >= size:
                break
            pos = max(pos, e)
            pos = (pos + align - 1) // align * align
        assert pos + size <= self.limit, (
            f"SBUF overflow: {name} needs {size}B at {pos} (limit {self.limit})")
        self.n += 1
        h = self.nc.alloc_sbuf_tensor_at(f"{name}_{self.n}", shape, dtype,
                                         offset=self.base + pos)
        self.blocks.append((pos, pos + size, h))
        return h

    def free(self, *hs):
        self.blocks = [b for b in self.blocks if b[2] not in hs]

    def used(self):
        return max((e for _, e, _ in self.blocks), default=0)


class Ctx:

    def __init__(self, name="mk"):
        nc = bass.Bass("TRN2", target_bir_lowering=False)
        self.nc = nc
        self.b = Builder(nc)
        self.sb = SB(nc)
        self.psf = [nc.alloc_psum_tensor(f"psf{i}", [128, 512], F32) for i in range(6)]
        self.psb = [nc.alloc_psum_tensor(f"psb{i}", [128, 1024], BF16) for i in range(2)]
        self.ident = self.sb.alloc("ident_bf", [128, 128], BF16)
        self.identf = self.sb.alloc("ident_f", [128, 128], F32)
        self.NCONST = 16
        self.cpool = self.sb.alloc("cpool", [128, self.NCONST], F32)
        self.cvals = []
        self.dram = {}

    def cf(self, v):
        v = float(v)
        if v not in self.cvals:
            assert len(self.cvals) < self.NCONST
            self.cvals.append(v)
        i = self.cvals.index(v)
        return self.cpool[:, i:i + 1]

    def const_array(self):
        a = np.zeros((128, self.NCONST), dtype=np.float32)
        a[:, :len(self.cvals)] = np.array(self.cvals, dtype=np.float32)[None, :]
        return a

    def din(self, name, shape, dtype):
        h = self.nc.dram_tensor(name, list(shape), dtype, kind="ExternalInput")
        self.dram[name] = h
        return h

    def dout(self, name, shape, dtype):
        h = self.nc.dram_tensor(name, list(shape), dtype, kind="ExternalOutput")
        self.dram[name] = h
        return h

    def dint(self, name, shape, dtype):
        h = self.nc.dram_tensor(name, list(shape), dtype, kind="Internal")
        self.dram[name] = h
        return h

    def load_constants(self):
        ib = self.din("ident_in_bf", [128, 128], BF16)
        if_ = self.din("ident_in_f", [128, 128], F32)
        cin = self.din("consts_f32", [128, self.NCONST], F32)
        self.b.dma("sync", self.ident[:, :], ib[:, :], sem="init")
        self.b.dma("sync", self.identf[:, :], if_[:, :], sem="init")
        t2 = self.b.dma("sync", self.cpool[:, :], cin[:, :], sem="init")
        self.t_const = t2
        return t2


def host_constants():
    import ml_dtypes
    return {
        "ident_in_bf": np.eye(128, dtype=ml_dtypes.bfloat16),
        "ident_in_f": np.eye(128, dtype=np.float32),
    }


def emit_entry(ctx, x_tm, xn_out, tag, tok_n=TOK, t_ident=None):
    b, sb = ctx.b, ctx.sb
    xbuf = [sb.alloc(f"en_x{i}_{tag}", [128, D], F32) for i in range(2)]
    xnb = [sb.alloc(f"en_xn{i}_{tag}", [128, D], BF16) for i in range(2)]
    evb = [sb.alloc(f"en_ev{i}_{tag}", [128, D], BF16) for i in range(2)]
    st = sb.alloc(f"en_st_{tag}", [128, 8], F32)
    trash = sb.alloc(f"en_trash_{tag}", [128, D], F32)

    n_tiles = tok_n // 128
    free_x = [None, None]
    free_xs = [None, None]
    free_xn = [None, None]
    free_ev = [None, None]
    free_ps = [None, None]
    prev_var = None
    prev_rc = None
    last = None
    for it in range(n_tiles):
        pb = it % 2
        xt, xnt, evt = xbuf[pb], xnb[pb], evb[pb]
        t_ld = b.dma("sync", xt[:, :], x_tm[it * 128:(it + 1) * 128, :],
                     sem=f"en_x{pb}_{tag}", waits=(free_x[pb], free_xs[pb]))
        t_sq = b.act("activation", trash[:, :], xt[:, :], AF.Square,
                     accum_out=st[:, 1:2], waits=(t_ld, prev_var))
        free_xs[pb] = t_sq
        t_r0 = b.dve("tensor_reduce", st[:, 0:1], xt[:, :], AX, ALU.add, waits=(t_ld,))
        t_r1 = b.dve("tensor_scalar_mul", st[:, 0:1], st[:, 0:1], 1.0 / D,
                     waits=(t_r0,))
        t_r2 = b.dve("tensor_mul", st[:, 2:3], st[:, 0:1], st[:, 0:1], waits=(t_r1,))
        t_var = b.dve("scalar_tensor_tensor", st[:, 3:4], st[:, 1:2], 1.0 / D,
                      st[:, 2:3], ALU.mult, ALU.subtract, waits=(t_sq, t_r2))
        prev_var = t_var
        t_rs = b.act("activation", st[:, 4:5], st[:, 3:4], AF.Sqrt, bias=ctx.cf(EPS),
                     waits=(t_var, prev_rc, t_ident if it == 0 else None))
        t_rc = b.dve("reciprocal", st[:, 5:6], st[:, 4:5], waits=(t_rs,))
        prev_rc = t_rc
        t_xn = b.dve("tensor_scalar", xnt[:, :], xt[:, :], st[:, 0:1], st[:, 5:6],
                     ALU.subtract, ALU.mult, waits=(t_rc, free_xn[pb]))
        free_x[pb] = t_xn
        for half in range(2):
            pbank = ctx.psb[half]
            w0 = [t_xn, free_ps[half]] + ([t_ident] if (it == 0 and half == 0) else [])
            t_tr = None
            for j in range(8):
                dt = half * 8 + j
                t_tr = b.pe("transpose", pbank[:, j * 128:(j + 1) * 128],
                            xnt[:, dt * 128:(dt + 1) * 128], ctx.ident[:, :],
                            waits=tuple(w0))
                w0 = []
            t_ev = b.dve("tensor_copy", evt[:, half * 1024:(half + 1) * 1024],
                         pbank[:, :], waits=(t_tr, free_ev[pb] if half == 0 else None))
            free_ps[half] = t_ev
            if half == 1:
                free_xn[pb] = t_ev
        dst = xn_out[:, it * 128:(it + 1) * 128].rearrange("(nt p) c -> p nt c", p=128)
        src = evt[:, :].rearrange("p (nt c) -> p nt c", c=128)
        t_st = b.dma("sync", dst, src, sem=f"en_s{pb}_{tag}", waits=(free_ps[1],))
        free_ev[pb] = t_st
        last = t_st
    sb.free(*xbuf, *xnb, *evb, st, trash)
    return last


def wgrid_dst(wblk, na):
    return wblk[:, :].rearrange("p (na c) -> p na c", c=128)


def wgrid_src(w_dram, out0, width=128):
    return w_dram[:, out0:out0 + width].rearrange("(na p) c -> p na c", p=128)


def load_wblk(b, q, wblk, w_dram, ot, na, width=128, waits=(), sem=None):
    dst = wblk[:, 0:na * width].rearrange("p (na c) -> p na c", c=width)
    return b.dma(q, dst, wgrid_src(w_dram, ot * width, width), waits=waits, sem=sem)


def fm_dram_view(t, c0, ch, nt=NT):
    return t[:, c0:c0 + ch].rearrange("(nt p) c -> p nt c", p=128)


def fm_sb_view(sb_t, ch, nt=NT):
    return sb_t[:, 0:nt * ch].rearrange("p (nt c) -> p nt c", c=ch)


def emit_l0pass1(ctx, xn_fm, w, A_out, B_out, sf_out, CH=256):
    b, sb = ctx.b, ctx.sb
    NC = TOK // CH
    NH = H // 128
    wu = sb.alloc("p1_wu", [128, NT * NT * 128], BF16)
    wg1 = sb.alloc("p1_wg1", [128, NT * NH * 128], BF16)
    wg2 = sb.alloc("p1_wg2", [128, NH * NT * 128], BF16)
    for ot in range(NT):
        b.dma("gpsimd",
              wu[:, ot * NT * 128:(ot + 1) * NT * 128].rearrange("p (na c) -> p na c", c=128),
              wgrid_src(w["WuT"], ot * 128), sem="p1w")
    for ot in range(NH):
        b.dma("gpsimd", wg1[:, ot * NT * 128:(ot + 1) * NT * 128].rearrange(
            "p (na c) -> p na c", c=128), wgrid_src(w["Wg1T"], ot * 128), sem="p1w")
    for ot in range(NT):
        t_wg = b.dma("gpsimd", wg2[:, ot * NH * 128:(ot + 1) * NH * 128].rearrange(
            "p (na c) -> p na c", c=128), wgrid_src(w["Wg2T"], ot * 128), sem="p1w")
    bu_s = sb.alloc("p1_bu", [128, NT], F32)
    bv_s = sb.alloc("p1_bv", [128, NT], F32)
    bg1_s = sb.alloc("p1_bg1", [128, NH], F32)
    bg2_s = sb.alloc("p1_bg2", [128, NT], F32)
    b.dma("sync", bu_s[:, :], w["bu_r"][:, :], sem="p1b")
    b.dma("sync", bv_s[:, :], w["bv_r"][:, :], sem="p1b")
    b.dma("sync", bg1_s[:, :], w["bg1_r"][:, :], sem="p1b")
    t_bias = b.dma("sync", bg2_s[:, :], w["bg2_r"][:, :], sem="p1b")
    t_wload = ("p1w", b.cnt["p1w"])
    wvb = [sb.alloc(f"p1_wv{i}", [128, NT * 128], BF16) for i in range(2)]
    xnb = [sb.alloc(f"p1_xn{i}", [128, NT * CH], BF16) for i in range(2)]
    hb = sb.alloc("p1_h", [128, NH * CH], BF16)
    gt = sb.alloc("p1_gt", [128, NT * CH], F32)
    ug = sb.alloc("p1_ug", [128, NT * CH], F32)
    vg = sb.alloc("p1_vg", [128, NT * CH], F32)
    kc = sb.alloc("p1_kc", [128, NT * CH], F32)
    carry = sb.alloc("p1_carry", [128, NT], F32)
    t_cz = b.dve("memset", carry[:, :], 0.0)

    free_xn = [None, None]
    free_wv = [None, None]
    free_ps = [None] * 6
    free_ug = None
    free_vg = None
    free_kc = None
    free_gt = None
    free_h = None
    t_scan_prev = t_cz
    pfi = [0]

    def nxps(waits):
        i = pfi[0] % 6
        pfi[0] += 1
        return i, ctx.psf[i], [free_ps[i]] + list(waits)

    t_A_st = t_B_st = None
    for c in range(NC):
        pb = c % 2
        xnt = xnb[pb]
        t_xn = b.dma("sync", fm_sb_view(xnt, CH), fm_dram_view(xn_fm, c * CH, CH),
                     sem=f"p1x{pb}", waits=(free_xn[pb],))
        t_h = None
        for ht in range(NH):
            i, ps, ws = nxps([t_xn, t_wload if c == 0 else None,
                              free_h if ht == 0 else None])
            tmm = None
            for a in range(NT):
                tmm = b.pe("matmul", ps[:, 0:CH],
                           wg1[:, (ht * NT + a) * 128:(ht * NT + a + 1) * 128],
                           xnt[:, a * CH:(a + 1) * CH],
                           start=(a == 0), stop=(a == NT - 1),
                           waits=tuple(ws) if a == 0 else ())
            t_h = b.act("activation", hb[:, ht * CH:(ht + 1) * CH], ps[:, 0:CH],
                        AF.Relu, bias=bg1_s[:, ht:ht + 1],
                        waits=(tmm, t_bias if c == 0 and ht == 0 else None))
            free_ps[i] = t_h
        t_g = None
        for rt in range(NT):
            i, ps, ws = nxps([t_h, free_gt if rt == 0 else None])
            tmm = None
            for a in range(NH):
                tmm = b.pe("matmul", ps[:, 0:CH],
                           wg2[:, (rt * NH + a) * 128:(rt * NH + a + 1) * 128],
                           hb[:, a * CH:(a + 1) * CH],
                           start=(a == 0), stop=(a == NH - 1),
                           waits=tuple(ws) if a == 0 else ())
            t_g = b.act("activation", gt[:, rt * CH:(rt + 1) * CH], ps[:, 0:CH],
                        AF.Sigmoid, bias=bg2_s[:, rt:rt + 1], waits=(tmm,))
            free_ps[i] = t_g
        free_h = t_g
        t_ug_last = None
        for rt in range(NT):
            i, ps, ws = nxps([t_g, free_ug if rt == 0 else None])
            tmm = None
            for a in range(NT):
                tmm = b.pe("matmul", ps[:, 0:CH],
                           wu[:, (rt * NT + a) * 128:(rt * NT + a + 1) * 128],
                           xnt[:, a * CH:(a + 1) * CH],
                           start=(a == 0), stop=(a == NT - 1),
                           waits=tuple(ws) if a == 0 else ())
            t_ug_last = b.dve("scalar_tensor_tensor", ug[:, rt * CH:(rt + 1) * CH],
                              ps[:, 0:CH], bu_s[:, rt:rt + 1], gt[:, rt * CH:(rt + 1) * CH],
                              ALU.add, ALU.mult, waits=(tmm,))
            free_ps[i] = t_ug_last
        t_vg_last = None
        for rt in range(NT):
            wb = wvb[rt % 2]
            t_wv = load_wblk(b, "gpsimd", wb, w["WvT"], rt, NT,
                             waits=(free_wv[rt % 2],), sem=f"p1wv{rt % 2}")
            i, ps, ws = nxps([t_wv, free_vg if rt == 0 else None])
            tmm = None
            for a in range(NT):
                tmm = b.pe("matmul", ps[:, 0:CH],
                           wb[:, a * 128:(a + 1) * 128],
                           xnt[:, a * CH:(a + 1) * CH],
                           start=(a == 0), stop=(a == NT - 1),
                           waits=tuple(ws) if a == 0 else ())
            free_wv[rt % 2] = tmm
            t_vg_last = b.dve("scalar_tensor_tensor", vg[:, rt * CH:(rt + 1) * CH],
                              ps[:, 0:CH], bv_s[:, rt:rt + 1], gt[:, rt * CH:(rt + 1) * CH],
                              ALU.add, ALU.mult, waits=(tmm,))
            free_ps[i] = t_vg_last
        free_gt = t_vg_last
        free_xn[pb] = t_vg_last
        t_scan = None
        for rt in range(NT):
            t_scan = b.dve("tensor_tensor_scan", kc[:, rt * CH:(rt + 1) * CH],
                           ug[:, rt * CH:(rt + 1) * CH], ug[:, rt * CH:(rt + 1) * CH],
                           carry[:, rt:rt + 1], ALU.add, ALU.bypass,
                           waits=(t_scan_prev if rt == 0 else None,
                                  free_kc if rt == 0 else None,
                                  t_ug_last if rt == 0 else None))
            b.dve("tensor_copy", carry[:, rt:rt + 1],
                  kc[:, rt * CH + CH - 1:rt * CH + CH], waits=(t_scan,))
        free_ug = t_scan
        t_scan_prev = ("dve", b.cnt["dve"])
        t_B_st = b.dma("sync", fm_dram_view(B_out, c * CH, CH), fm_sb_view(vg, CH),
                       sem=f"p1B{c % 2}", waits=(t_vg_last,))
        t_A = b.dve("tensor_mul", kc[:, 0:NT * CH], kc[:, 0:NT * CH], vg[:, 0:NT * CH],
                    waits=(t_scan_prev,))
        free_vg = t_B_st
        t_A_st = b.dma("sync", fm_dram_view(A_out, c * CH, CH), fm_sb_view(kc, CH),
                       sem=f"p1A{c % 2}", waits=(t_A,))
        free_kc = t_A_st
    t_sf = b.dma("sync", sf_out[:, :], carry[:, :], sem="p1sf", waits=(t_scan_prev,))
    sb.free(wu, wg1, wg2, bu_s, bv_s, bg1_s, bg2_s, *wvb, *xnb, hb, gt, ug, vg,
            kc, carry)
    return t_A_st, t_B_st, t_sf


def emit_l0pass2(ctx, xn_fm, A_s, B_s, sf_partner, mask, w, delta_out, CH=256):
    b, sb = ctx.b, ctx.sb
    NC = TOK // CH
    CH2 = CH + 2
    wu = sb.alloc("p2_wu", [128, NT * NT * 128], BF16)
    for ot in range(NT):
        b.dma("gpsimd",
              wu[:, ot * NT * 128:(ot + 1) * NT * 128].rearrange("p (na c) -> p na c", c=128),
              wgrid_src(w["WuT"], ot * 128), sem="p2w")
    t_wload = ("p2w", b.cnt["p2w"])
    prefix = sb.alloc("p2_prefix", [128, NT], F32)
    msk = sb.alloc("p2_msk", [128, 1], F32)
    cw = sb.alloc("p2_cw", [128, 3 * NT], F32)
    cb = sb.alloc("p2_cb", [128, NT], F32)
    b.dma("scalar", prefix[:, :], sf_partner[:, :], sem="p2c")
    b.dma("scalar", msk[:, :], mask[:, :], sem="p2c")
    b.dma("scalar", cw[:, :], w["convw_r"][:, :], sem="p2c")
    t_cbl = b.dma("scalar", cb[:, :], w["convb_r"][:, :], sem="p2c")
    t_pref = b.dve("tensor_scalar_mul", prefix[:, :], prefix[:, :], msk[:, 0:1],
                   waits=(t_cbl,))

    halo = sb.alloc("p2_halo", [128, 2 * NT], BF16)
    hx = sb.alloc("p2_hx", [1, D], F32)
    hxn = sb.alloc("p2_hxn", [1, D], BF16)
    hsq = sb.alloc("p2_hsq", [1, D], F32)
    hst = sb.alloc("p2_hst", [1, 8], F32)
    t_hev = None
    for side, name in ((0, "xhalo_l"), (1, "xhalo_r")):
        t_hl = b.dma("scalar", hx[:, :], w[name][:, :], sem="p2hx",
                     waits=(t_hev,))
        t_sq = b.act("activation", hsq[:, :], hx[:, :], AF.Square,
                     accum_out=hst[:, 1:2], waits=(t_hl, t_hev))
        t_r0 = b.dve("tensor_reduce", hst[:, 0:1], hx[:, :], AX, ALU.add,
                     waits=(t_hl,))
        t_r1 = b.dve("tensor_scalar_mul", hst[:, 0:1], hst[:, 0:1], 1.0 / D,
                     waits=(t_r0,))
        t_r2 = b.dve("tensor_mul", hst[:, 2:3], hst[:, 0:1], hst[:, 0:1],
                     waits=(t_r1,))
        t_var = b.dve("scalar_tensor_tensor", hst[:, 3:4], hst[:, 1:2], 1.0 / D,
                      hst[:, 2:3], ALU.mult, ALU.subtract, waits=(t_sq, t_r2))
        t_rs = b.act("activation", hst[:, 4:5], hst[:, 3:4], AF.Sqrt,
                     bias=ctx.cf(EPS)[0:1, :], waits=(t_var,))
        t_rc = b.dve("reciprocal", hst[:, 5:6], hst[:, 4:5], waits=(t_rs,))
        t_xn = b.dve("tensor_scalar", hxn[:, :], hx[:, :], hst[:, 0:1],
                     hst[:, 5:6], ALU.subtract, ALU.mult, waits=(t_rc,))
        ps = ctx.psf[5]
        tmm = None
        for dt in range(NT):
            tmm = b.pe("matmul", ps[:, dt:dt + 1],
                       hxn[:, dt * 128:(dt + 1) * 128], ctx.ident[0:1, 0:1],
                       start=True, stop=True,
                       waits=(t_xn, t_hev, ctx.t_const) if dt == 0 else ())
        t_hev = b.dve("tensor_copy", halo[:, side * NT:(side + 1) * NT],
                      ps[:, 0:NT], waits=(tmm,))

    Ab = sb.alloc("p2_A", [128, NT * CH], F32)
    Bb = sb.alloc("p2_B", [128, NT * CH], F32)
    Ab2 = sb.alloc("p2_A2", [128, NT * CH], F32)
    Bb2 = sb.alloc("p2_B2", [128, NT * CH], F32)
    xq = sb.alloc("p2_xq", [128, NT * CH2], BF16)
    xq2 = sb.alloc("p2_xq2", [128, NT * CH2], BF16)
    gl = sb.alloc("p2_gl", [128, NT * CH], BF16)
    cv = sb.alloc("p2_cv", [128, CH], F32)
    dl = sb.alloc("p2_dl", [128, NT * CH], F32)
    free_A = [None, None]
    free_B = [None, None]
    free_xq = [None, None]
    free_gl = None
    free_dl = None
    free_ps = [None] * 6
    free_ps[5] = t_hev
    pfi = [0]

    def nxps(waits):
        i = pfi[0] % 6
        pfi[0] += 1
        return i, ctx.psf[i], [free_ps[i]] + list(waits)

    t_d_st = None
    for c in range(NC):
        pb = c % 2
        At, Bt, xqt = (Ab, Bb, xq) if pb == 0 else (Ab2, Bb2, xq2)
        t_A = b.dma("sync", fm_sb_view(At, CH), fm_dram_view(A_s, c * CH, CH),
                    sem=f"p2A{pb}", waits=(free_A[pb],))
        t_B = b.dma("sync", fm_sb_view(Bt, CH), fm_dram_view(B_s, c * CH, CH),
                    sem=f"p2B{pb}", waits=(free_B[pb],))
        xqv = xqt[:, :].rearrange("p (nt c) -> p nt c", c=CH2)
        lo = 1 if c == 0 else 0
        hi = CH + 1 if c == NC - 1 else CH + 2
        t_x = b.dma("sync", xqv[:, :, lo:hi],
                    fm_dram_view(xn_fm, c * CH - 1 + lo, hi - lo),
                    sem=f"p2x{pb}", waits=(free_xq[pb],))
        t_xl = t_xr = None
        t_hcp = None
        if c == 0:
            for nt in range(NT):
                t_hcp = b.dve("tensor_copy", xqt[:, nt * CH2:nt * CH2 + 1],
                              halo[:, nt:nt + 1],
                              waits=(t_hev, t_x, free_xq[pb]) if nt == 0 else ())
        if c == NC - 1:
            for nt in range(NT):
                t_hcp = b.dve("tensor_copy",
                              xqt[:, nt * CH2 + CH + 1:nt * CH2 + CH + 2],
                              halo[:, NT + nt:NT + nt + 1],
                              waits=(t_hev, t_x, free_xq[pb]) if nt == 0 else ())
        if c == 0 and "dbg_xq" in w:
            tdbg = b.dma("gpsimd", w["dbg_xq"][:, :], xqt[:, 0:2 * CH2],
                         waits=(t_hcp, t_x), sem="dbg")
            b.dma("gpsimd", w["dbg_halo"][:, :], halo[:, :], waits=(t_hev,), sem="dbg")
        dbg_gl_pending = (c == 0 and "dbg_gl" in w)
        t_gl = None
        for rt in range(NT):
            t_gl = b.dve("scalar_tensor_tensor", gl[:, rt * CH:(rt + 1) * CH],
                         Bt[:, rt * CH:(rt + 1) * CH], prefix[:, rt:rt + 1],
                         At[:, rt * CH:(rt + 1) * CH], ALU.mult, ALU.add,
                         waits=(t_A, t_B, t_pref, free_gl) if rt == 0 else ())
        if c == 0 and "dbg_gl" in w:
            b.dma("gpsimd", w["dbg_gl"][:, :], gl[:, :], waits=(t_gl,), sem="dbg")
        free_A[pb] = t_gl
        free_B[pb] = t_gl
        t_dl = None
        for dt in range(NT):
            i, ps, ws = nxps([t_gl, t_wload if c == 0 and dt == 0 else None,
                              free_dl if dt == 0 else None])
            tmm = None
            for a in range(NT):
                tmm = b.pe("matmul", ps[:, 0:CH],
                           wu[:, (dt * NT + a) * 128:(dt * NT + a + 1) * 128],
                           gl[:, a * CH:(a + 1) * CH],
                           start=(a == 0), stop=(a == NT - 1),
                           waits=tuple(ws) if a == 0 else ())
            if dt == NT - 1:
                free_gl = tmm
            base = dt * CH2
            t_c0 = b.dve("tensor_scalar_mul", cv[:, 0:CH],
                         xqt[:, base:base + CH], cw[:, 3 * dt:3 * dt + 1],
                         waits=(t_x, t_xl, t_xr, t_hcp, t_cbl) if dt == 0 else ())
            t_c1 = b.dve("scalar_tensor_tensor", cv[:, 0:CH],
                         xqt[:, base + 1:base + 1 + CH], cw[:, 3 * dt + 1:3 * dt + 2],
                         cv[:, 0:CH], ALU.mult, ALU.add, waits=(t_c0,))
            t_c2 = b.dve("scalar_tensor_tensor", cv[:, 0:CH],
                         xqt[:, base + 2:base + 2 + CH], cw[:, 3 * dt + 2:3 * dt + 3],
                         cv[:, 0:CH], ALU.mult, ALU.add, waits=(t_c1,))
            t_dl = b.dve("scalar_tensor_tensor", dl[:, dt * CH:(dt + 1) * CH],
                         ps[:, 0:CH], cb[:, dt:dt + 1], cv[:, 0:CH],
                         ALU.add, ALU.add, waits=(tmm, t_c2, free_dl if dt == 0 else None))
            free_ps[i] = t_dl
            if dt == NT - 1:
                free_xq[pb] = t_dl
        t_d_st = b.dma("sync", fm_dram_view(delta_out, c * CH, CH),
                       fm_sb_view(dl, CH), sem=f"p2d{c % 2}", waits=(t_dl,))
        free_dl = t_d_st
    sb.free(wu, prefix, msk, cw, cb, halo, Ab, Bb, Ab2, Bb2, xq, xq2, gl, cv, dl,
            hx, hxn, hsq, hst)
    return t_d_st


def emit_quant(ctx, wT, wqT, IN, OUT, tag):
    b, sb = ctx.b, ctx.sb
    NB = IN // 128
    wb = [sb.alloc(f"q_w{i}_{tag}", [128, OUT], F32) for i in range(2)]
    acc = sb.alloc(f"q_acc_{tag}", [128, 4], F32)
    t1 = [sb.alloc(f"q_t1{i}_{tag}", [128, OUT], BF16) for i in range(2)]
    t2 = sb.alloc(f"q_t2_{tag}", [128, OUT], BF16)
    ones_row = sb.alloc(f"q_ones_{tag}", [1, 128], F32)
    tq = sb.alloc(f"q_tq_{tag}", [128, 2], F32)
    b.dve("memset", ones_row[:, :], 1.0)
    t_az = b.dve("memset", acc[:, :], 0.0)
    free_w = [None, None]
    t_red = None
    for i in range(NB):
        pb = i % 2
        t_ld = b.dma("sync", wb[pb][:, :], wT[i * 128:(i + 1) * 128, :],
                     sem=f"qw{pb}_{tag}", waits=(free_w[pb],))
        t_r = b.dve("tensor_reduce", acc[:, 1:2], wb[pb][:, :], AX, ALU.add,
                    apply_absolute_value=True, waits=(t_ld, t_az if i == 0 else None))
        t_red = b.dve("tensor_add", acc[:, 0:1], acc[:, 0:1], acc[:, 1:2],
                      waits=(t_r,))
        free_w[pb] = t_red
    ps = ctx.psf[0]
    t_mm = b.pe("matmul", ps[0:1, 0:1], acc[:, 0:1], ctx.cf(1.0),
                start=True, stop=True, waits=(t_red, ctx.t_const))
    t_sc = b.dve("tensor_scalar_mul", acc[0:1, 2:3], ps[0:1, 0:1],
                 0.7 / (IN * OUT), waits=(t_mm,))
    t_bc = b.pe("matmul", ps[:, 4:5], ones_row[:, :], acc[0:1, 2:3],
                start=True, stop=True, waits=(t_sc,))
    t_tq = b.dve("tensor_copy", tq[:, 0:1], ps[:, 4:5], waits=(t_bc,))
    t_tqn = b.dve("tensor_scalar_mul", tq[:, 1:2], tq[:, 0:1], -1.0,
                  waits=(t_tq,))
    t_st = None
    free_st = [None, None]
    for i in range(NB):
        pb = i % 2
        t_ld = b.dma("sync", wb[pb][:, :], wT[i * 128:(i + 1) * 128, :],
                     sem=f"qw{pb}_{tag}", waits=(free_w[pb],))
        t_m1 = b.dve("tensor_single_scalar", t1[pb][:, :], wb[pb][:, :], tq[:, 0:1],
                     ALU.is_ge, waits=(t_ld, t_tqn if i == 0 else None,
                                       free_st[pb]))
        t_m2 = b.dve("tensor_single_scalar", t2[:, :], wb[pb][:, :], tq[:, 1:2],
                     ALU.is_le, waits=(t_m1,))
        t_q = b.dve("tensor_sub", t1[pb][:, :], t1[pb][:, :], t2[:, :], waits=(t_m2,))
        free_w[pb] = t_q
        t_st = b.dma("sync", wqT[i * 128:(i + 1) * 128, :], t1[pb][:, :],
                     sem=f"qs{pb}_{tag}", waits=(t_q,))
        free_st[pb] = t_st
    sb.free(*wb, acc, *t1, t2, ones_row, tq)
    return t_st


def emit_l0ffn(ctx, xn_fm, w1qT, w2qT, b1_r, b2_r, delta_in, delta_out, CH=512):
    b, sb = ctx.b, ctx.sb
    NC = TOK // CH
    NF = FF // 128
    xnb = [sb.alloc(f"ff_xn{i}", [128, NT * CH], BF16) for i in range(2)]
    hb = sb.alloc("ff_h", [128, NF * CH], BF16)
    w1b = [sb.alloc(f"ff_w1{i}", [128, NT * 128], BF16) for i in range(2)]
    w2b = [sb.alloc(f"ff_w2{i}", [128, NF * 128], BF16) for i in range(2)]
    dlb = [sb.alloc(f"ff_dl{i}", [128, NT * CH], F32) for i in range(2)]
    b1s = sb.alloc("ff_b1", [128, NF], F32)
    b2s = sb.alloc("ff_b2", [128, NT], F32)
    b.dma("scalar", b1s[:, :], b1_r[:, :], sem="ffb")
    t_b = b.dma("scalar", b2s[:, :], b2_r[:, :], sem="ffb")
    free_xn = [None, None]
    free_w1 = [None, None]
    free_w2 = [None, None]
    free_dl = [None, None]
    free_ps = [None] * 6
    free_h = None
    pfi = [0]

    def nxps(waits):
        i = pfi[0] % 6
        pfi[0] += 1
        return i, ctx.psf[i], [free_ps[i]] + list(waits)

    t_st = None
    for c in range(NC):
        pb = c % 2
        xnt, dlt = xnb[pb], dlb[pb]
        t_xn = b.dma("sync", fm_sb_view(xnt, CH), fm_dram_view(xn_fm, c * CH, CH),
                     sem=f"ffx{pb}", waits=(free_xn[pb],))
        t_dli = b.dma("sync", fm_sb_view(dlt, CH), fm_dram_view(delta_in, c * CH, CH),
                      sem=f"ffd{pb}", waits=(free_dl[pb],))
        t_h = None
        for ft in range(NF):
            wpb = ft % 2
            t_w1 = load_wblk(b, "sync", w1b[wpb], w1qT, ft, NT,
                             waits=(free_w1[wpb],), sem=f"ffw1{wpb}")
            i, ps, ws = nxps([t_w1, t_xn, t_b if c == 0 and ft == 0 else None,
                              free_h if ft == 0 else None])
            ncol = CH // 512
            tmm = None
            for col in range(ncol):
                for a in range(NT):
                    tmm = b.pe("matmul", ps[:, col * 512:(col + 1) * 512],
                               w1b[wpb][:, a * 128:(a + 1) * 128],
                               xnt[:, a * CH + col * 512:a * CH + (col + 1) * 512],
                               start=(a == 0), stop=(a == NT - 1),
                               waits=tuple(ws) if (a == 0 and col == 0) else ())
            free_w1[wpb] = tmm
            t_h = b.act("activation", hb[:, ft * CH:(ft + 1) * CH], ps[:, 0:CH],
                        AF.Gelu, bias=b1s[:, ft:ft + 1], waits=(tmm,))
            free_ps[i] = t_h
        t_dl = None
        for dt in range(NT):
            wpb = dt % 2
            t_w2 = load_wblk(b, "sync", w2b[wpb], w2qT, dt, NF,
                             waits=(free_w2[wpb],), sem=f"ffw2{wpb}")
            i, ps, ws = nxps([t_w2, t_h])
            ncol = CH // 512
            tmm = None
            for col in range(ncol):
                for a in range(NF):
                    tmm = b.pe("matmul", ps[:, col * 512:(col + 1) * 512],
                               w2b[wpb][:, a * 128:(a + 1) * 128],
                               hb[:, a * CH + col * 512:a * CH + (col + 1) * 512],
                               start=(a == 0), stop=(a == NF - 1),
                               waits=tuple(ws) if (a == 0 and col == 0) else ())
            free_w2[wpb] = tmm
            if dt == NT - 1:
                free_h = tmm
            t_dl = b.dve("scalar_tensor_tensor", dlt[:, dt * CH:(dt + 1) * CH],
                         ps[:, 0:CH], b2s[:, dt:dt + 1], dlt[:, dt * CH:(dt + 1) * CH],
                         ALU.add, ALU.add, waits=(tmm, t_dli if dt == 0 else None))
            free_ps[i] = t_dl
        t_st = b.dma("sync", fm_dram_view(delta_out, c * CH, CH),
                     fm_sb_view(dlt, CH), sem=f"ffs{pb}", waits=(t_dl,))
        free_dl[pb] = t_st
        free_xn[pb] = t_dl
    sb.free(*xnb, hb, *w1b, *w2b, *dlb, b1s, b2s)
    return t_st


def emit_exit(ctx, x_tm, delta_fm, out_tm, tag, tok_n=TOK):
    b, sb = ctx.b, ctx.sb
    dlb = [sb.alloc(f"ex_dl{i}_{tag}", [128, D], F32) for i in range(2)]
    xb = [sb.alloc(f"ex_x{i}_{tag}", [128, D], F32) for i in range(2)]
    ob = [sb.alloc(f"ex_o{i}_{tag}", [128, D], F32) for i in range(2)]
    n_tiles = tok_n // 128
    free_dl = [None, None]
    free_x = [None, None]
    free_o = [None, None]
    free_ps = [None, None]
    last = None
    for it in range(n_tiles):
        pb = it % 2
        dlt, xt, ot = dlb[pb], xb[pb], ob[pb]
        t_dl = b.dma("sync", fm_sb_view(dlt, 128), fm_dram_view(delta_fm, it * 128, 128),
                     sem=f"exd{pb}_{tag}", waits=(free_dl[pb],))
        t_x = b.dma("sync", xt[:, :], x_tm[it * 128:(it + 1) * 128, :],
                    sem=f"exx{pb}_{tag}", waits=(free_x[pb],))
        t_add = None
        for g in range(4):
            ps = ctx.psf[g % 2]
            tmm = None
            for j in range(4):
                dt = g * 4 + j
                tmm = b.pe("transpose", ps[:, j * 128:(j + 1) * 128],
                           dlt[:, dt * 128:(dt + 1) * 128], ctx.identf[:, :],
                           waits=(t_dl, free_ps[g % 2], ctx.t_const if it == 0 and g == 0 else None)
                           if j == 0 else ())
            t_add = b.dve("tensor_add", ot[:, g * 512:(g + 1) * 512], ps[:, 0:512],
                          xt[:, g * 512:(g + 1) * 512],
                          waits=(tmm, t_x if g == 0 else None,
                                 free_o[pb] if g == 0 else None))
            free_ps[g % 2] = t_add
        free_dl[pb] = t_add
        free_x[pb] = t_add
        t_st = b.dma("sync", out_tm[it * 128:(it + 1) * 128, :], ot[:, :],
                     sem=f"exs{pb}_{tag}", waits=(t_add,))
        free_o[pb] = t_st
        last = t_st
    sb.free(*dlb, *xb, *ob)
    return last


def emit_l1proj(ctx, s_fm, wq, wk, wv, bq_r, bk_r, bv_row, Q_out, K_out, V_out,
                li):
    b, sb = ctx.b, ctx.sb
    s_sb = sb.alloc(f"pj_s_{li}", [128, NT * TOK], BF16)
    t_s = b.dma("sync", s_sb[:, :].rearrange("p (nt c) -> p nt c", c=TOK),
                fm_dram_view(s_fm, 0, TOK), sem="pjs")
    wb = [sb.alloc(f"pj_w{i}_{li}", [128, NT * 128], BF16) for i in range(2)]
    wvb = [sb.alloc(f"pj_wv{i}_{li}", [128, NT * 512], BF16) for i in range(2)]
    stg = [sb.alloc(f"pj_st{i}_{li}", [128, 512], BF16) for i in range(4)]
    bqs = sb.alloc(f"pj_bq_{li}", [128, NT], BF16)
    bks = sb.alloc(f"pj_bk_{li}", [128, NT], BF16)
    bvs = sb.alloc(f"pj_bv_{li}", [1, D], BF16)
    ones_bf = sb.alloc(f"pj_one_{li}", [1, 128], BF16)
    t_ob = b.dve("memset", ones_bf[:, :], 1.0)
    b.dma("scalar", bqs[:, :], bq_r[:, :], sem="pjb")
    b.dma("scalar", bks[:, :], bk_r[:, :], sem="pjb")
    t_b = b.dma("scalar", bvs[:, :], bv_row[:, :], sem="pjb")
    free_w = [None, None]
    free_wv = [None, None]
    free_stg = [None] * 4
    free_ps = [None] * 6
    pfi = [0]
    si = [0]

    def nxps(waits):
        i = pfi[0] % 6
        pfi[0] += 1
        return i, ctx.psf[i], [free_ps[i]] + list(waits)

    def nxstg():
        i = si[0] % 4
        si[0] += 1
        return i, stg[i]

    for (wdr, bias, outd, nm) in ((wq, bqs, Q_out, "q"), (wk, bks, K_out, "k")):
        for ot in range(NT):
            pb = ot % 2
            t_w = load_wblk(b, "sync", wb[pb], wdr, ot, NT,
                            waits=(free_w[pb],), sem=f"pjw{pb}")
            for tc in range(TOK // 512):
                i, ps, ws = nxps([t_w, t_s if (ot == 0 and tc == 0) else None])
                tmm = None
                for a in range(NT):
                    tmm = b.pe("matmul", ps[:, 0:512],
                               wb[pb][:, a * 128:(a + 1) * 128],
                               s_sb[:, a * TOK + tc * 512:a * TOK + (tc + 1) * 512],
                               start=(a == 0), stop=(a == NT - 1),
                               waits=tuple(ws) if a == 0 else ())
                if tc == TOK // 512 - 1:
                    free_w[pb] = tmm
                j, st = nxstg()
                t_ev = b.act("activation", st[:, :], ps[:, 0:512], AF.Identity,
                             bias=bias[:, ot:ot + 1],
                             waits=(tmm, free_stg[j],
                                    t_b if (ot == 0 and tc == 0) else None))
                free_ps[i] = t_ev
                t_st = b.dma("sync", outd[ot * 128:(ot + 1) * 128, tc * 512:(tc + 1) * 512],
                             st[:, :], sem=f"pjo{j}", waits=(t_ev,))
                free_stg[j] = t_st
    for dvb in range(4):
        pb = dvb % 2
        t_w = load_wblk(b, "sync", wvb[pb], wv, dvb, NT, width=512,
                        waits=(free_wv[pb],), sem=f"pjwv{pb}")
        for tt in range(TOK // 128):
            i, ps, ws = nxps([t_w, t_ob if (dvb == 0 and tt == 0) else None])
            tmm = None
            for a in range(NT):
                tmm = b.pe("matmul", ps[:, 0:512],
                           s_sb[:, a * TOK + tt * 128:a * TOK + (tt + 1) * 128],
                           wvb[pb][:, a * 512:(a + 1) * 512],
                           start=(a == 0), stop=False,
                           waits=tuple(ws) if a == 0 else ())
            tmm = b.pe("matmul", ps[:, 0:512], ones_bf[:, :],
                       bvs[:, dvb * 512:(dvb + 1) * 512],
                       start=False, stop=True)
            if tt == TOK // 128 - 1:
                free_wv[pb] = tmm
            j, st = nxstg()
            t_ev = b.dve("tensor_copy", st[:, :], ps[:, 0:512],
                         waits=(tmm, free_stg[j]))
            free_ps[i] = t_ev
            t_st = b.dma("sync", V_out[tt * 128:(tt + 1) * 128, dvb * 512:(dvb + 1) * 512],
                         st[:, :], sem=f"pjo{j}", waits=(t_ev,))
            free_stg[j] = t_st
    sb.free(s_sb, *wb, *wvb, *stg, bqs, bks, bvs, ones_bf)
    return free_stg


def emit_l1attn(ctx, Q_dram, K_dram, V_dram, wo, bo_r, out_dram, li,
                out_f32=False):
    b, sb = ctx.b, ctx.sb
    KS = 512
    NKS = KTOT // KS
    QG = 1024
    NQG = TOK // QG
    NQT = QG // 128
    q_sb = sb.alloc(f"at_q_{li}", [128, NT * QG], BF16)
    atT = q_sb
    k_sb = [sb.alloc(f"at_k{i}_{li}", [128, NT * KS], BF16) for i in range(2)]
    v_sb = [sb.alloc(f"at_v{i}_{li}", [128, (KS // 128) * D], BF16) for i in range(2)]
    acc = sb.alloc(f"at_acc_{li}", [128, NQT * D], F32)
    prb = [sb.alloc(f"at_p{i}_{li}", [128, KS], BF16) for i in range(2)]
    prT = [sb.alloc(f"at_pt{i}_{li}", [128, KS], BF16) for i in range(2)]
    ob_sb = [sb.alloc(f"at_ob{i}_{li}", [128, D], BF16) for i in range(2)]
    wob = [sb.alloc(f"at_wo{i}_{li}", [128, NT * 128], BF16) for i in range(2)]
    stg = [sb.alloc(f"at_st{i}_{li}", [128, 512], BF16 if not out_f32 else F32)
           for i in range(2)]
    bos = sb.alloc(f"at_bo_{li}", [128, NT], BF16)
    st = sb.alloc(f"at_state_{li}", [128, 8 * NQT], F32)
    t_bo = b.dma("scalar", bos[:, :], bo_r[:, :], sem="atb")

    def S(qt, k):
        return st[:, qt * 8 + k:qt * 8 + k + 1]

    free_q = None
    free_k = [None, None]
    free_v = [None, None]
    free_ps = [None] * 6
    free_tb = None
    free_fb = None
    free_prb = [None, None]
    free_prT = [None, None]
    free_acc = [None] * NQT
    free_st8 = [None] * NQT
    free_wo = [None, None]
    free_stg = [None, None]
    free_ob = [None, None]
    last_store = None
    for qg in range(NQG):
        t_q = b.dma("sync", q_sb[:, :].rearrange("p (nt c) -> p nt c", c=QG),
                    fm_dram_view(Q_dram, qg * QG, QG, nt=NT),
                    sem=f"atq{qg % 2}", waits=(free_q,))
        t_z_arr = [None] * NQT
        for ks in range(NKS):
            kpb = ks % 2
            half = ks // (NKS // 2)
            kcol = (ks % (NKS // 2)) * KS
            t_k = b.dma("sync",
                        k_sb[kpb][:, :].rearrange("p (nt c) -> p nt c", c=KS),
                        K_dram[half * D:(half + 1) * D, kcol:kcol + KS]
                        .rearrange("(nt p) c -> p nt c", p=128),
                        sem=f"atk{kpb}", waits=(free_k[kpb],))
            t_v = b.dma("sync",
                        v_sb[kpb][:, :].rearrange("p (nk c) -> p nk c", c=D),
                        V_dram[half * (KTOT // 2) + kcol:
                               half * (KTOT // 2) + kcol + KS, :]
                        .rearrange("(nk p) c -> p nk c", p=128),
                        sem=f"atv{kpb}", waits=(free_v[kpb],))
            for qt in range(NQT):
                spb = qt % 2
                ps_s = ctx.psf[spb]
                tmm = None
                for a in range(NT):
                    tmm = b.pe("matmul", ps_s[:, 0:KS],
                               q_sb[:, a * QG + qt * 128:a * QG + (qt + 1) * 128],
                               k_sb[kpb][:, a * KS:(a + 1) * KS],
                               start=(a == 0), stop=(a == NT - 1),
                               waits=(t_k, t_q, free_ps[spb]) if a == 0 else ())
                t_bm = b.dve("tensor_reduce", S(qt, 2), ps_s[:, 0:KS], AX,
                             ALU.max, waits=(tmm, free_st8[qt] if ks == 0 else None))
                if ks == 0:
                    t_mn = b.dve("tensor_copy", S(qt, 0), S(qt, 2), waits=(t_bm,))
                    t_al = None
                else:
                    t_mn = b.dve("tensor_max", S(qt, 2), S(qt, 2), S(qt, 0),
                                 waits=(t_bm,))
                    t_dm = b.dve("tensor_sub", S(qt, 3), S(qt, 0), S(qt, 2),
                                 waits=(t_mn,))
                    t_dm2 = b.dve("tensor_scalar_mul", S(qt, 3), S(qt, 3), SCALE,
                                  waits=(t_dm,))
                    t_cp = b.dve("tensor_copy", S(qt, 0), S(qt, 2), waits=(t_dm2,))
                    t_al = b.act("activation", S(qt, 4), S(qt, 3), AF.Exp,
                                 waits=(t_dm2,))
                t_bn = b.dve("tensor_scalar_mul", S(qt, 5), S(qt, 0), -SCALE,
                             waits=(t_mn if ks == 0 else t_cp,))
                t_pr = b.act("activation", prb[spb][:, :], ps_s[:, 0:KS], AF.Exp,
                             bias=S(qt, 5), scale=ctx.cf(SCALE),
                             accum_out=S(qt, 6),
                             waits=(t_bn, free_prb[spb], ctx.t_const if qg == 0 and ks == 0 and qt == 0 else None))
                free_ps[spb] = t_pr
                if ks == 0:
                    t_z = b.dve("tensor_copy", S(qt, 1), S(qt, 6), waits=(t_pr,))
                else:
                    t_z = b.dve("scalar_tensor_tensor", S(qt, 1), S(qt, 1),
                                S(qt, 4), S(qt, 6), ALU.mult, ALU.add,
                                waits=(t_pr, t_al))
                t_z_arr[qt] = t_z
                ttr = None
                for j in range(4):
                    ttr = b.pe("transpose", ctx.psb[0][:, j * 128:(j + 1) * 128],
                               prb[spb][:, j * 128:(j + 1) * 128], ctx.ident[:, :],
                               waits=(t_pr, free_tb) if j == 0 else ())
                free_prb[spb] = ttr
                t_pt = b.dve("tensor_copy", prT[spb][:, :], ctx.psb[0][:, 0:KS],
                             waits=(ttr, free_prT[spb]))
                free_tb = t_pt
                pv_mm = [None] * 4
                for dvb in range(4):
                    ps_p = ctx.psf[2 + dvb]
                    tpv = None
                    for j in range(4):
                        tpv = b.pe("matmul", ps_p[:, 0:512],
                                   prT[spb][:, j * 128:(j + 1) * 128],
                                   v_sb[kpb][:, j * D + dvb * 512:j * D + (dvb + 1) * 512],
                                   start=(j == 0), stop=(j == 3),
                                   waits=(t_pt, t_v, free_ps[2 + dvb]) if j == 0 else ())
                    pv_mm[dvb] = tpv
                free_prT[spb] = pv_mm[3]
                for dvb in range(4):
                    a_sl = acc[:, qt * D + dvb * 512:qt * D + (dvb + 1) * 512]
                    if ks == 0:
                        t_ac = b.dve("tensor_copy", a_sl, ctx.psf[2 + dvb][:, 0:512],
                                     waits=(pv_mm[dvb], free_acc[qt] if dvb == 0 else None))
                    else:
                        t_ac = b.dve("scalar_tensor_tensor", a_sl, a_sl, S(qt, 4),
                                     ctx.psf[2 + dvb][:, 0:512], ALU.mult, ALU.add,
                                     waits=(pv_mm[dvb], t_al))
                    free_ps[2 + dvb] = t_ac
                if qt == NQT - 1:
                    free_k[kpb] = tmm
                    free_v[kpb] = pv_mm[3]
        t_attnT = None
        for qt in range(NQT):
            opb = qt % 2
            t_rz = b.dve("reciprocal", S(qt, 7), S(qt, 1), waits=(t_z_arr[qt],))
            t_obv = b.dve("tensor_scalar_mul", ob_sb[opb][:, :],
                          acc[:, qt * D:(qt + 1) * D], S(qt, 7),
                          waits=(t_rz, free_ob[opb]))
            free_acc[qt] = t_obv
            free_st8[qt] = t_obv
            ttr = None
            for g in range(2):
                for j in range(8):
                    dv = g * 8 + j
                    ttr = b.pe("transpose", ctx.psb[1][:, j * 128:(j + 1) * 128],
                               ob_sb[opb][:, dv * 128:(dv + 1) * 128],
                               ctx.ident[:, :],
                               waits=(t_obv, free_fb) if j == 0 else ())
                t_ev = b.dve("tensor_copy", _atT_view(atT, QG, g, qt),
                             ctx.psb[1][:, 0:1024].rearrange(
                                 "p (j c) -> p j c", c=128), waits=(ttr,))
                free_fb = t_ev
            t_attnT = free_fb
            free_ob[opb] = ttr
        for ot in range(NT):
            wpb = ot % 2
            t_w = load_wblk(b, "sync", wob[wpb], wo, ot, NT,
                            waits=(free_wo[wpb],), sem=f"atw{wpb}")
            for hf in range(2):
                i = hf
                ps = ctx.psf[i]
                tmm = None
                for a in range(NT):
                    tmm = b.pe("matmul", ps[:, 0:512],
                               wob[wpb][:, a * 128:(a + 1) * 128],
                               atT[:, a * QG + hf * 512:a * QG + (hf + 1) * 512],
                               start=(a == 0), stop=(a == NT - 1),
                               waits=(t_w, t_attnT, free_ps[i]) if a == 0 else ())
                if hf == 1 and ot == NT - 1:
                    free_q = tmm
                j = (ot * 2 + hf) % 2
                t_ev = b.act("activation", stg[j][:, :], ps[:, 0:512], AF.Identity,
                             bias=bos[:, ot:ot + 1],
                             waits=(tmm, free_stg[j],
                                    t_bo if qg == 0 and ot == 0 and hf == 0 else None))
                free_ps[i] = t_ev
                t_st = b.dma("sync",
                             out_dram[ot * 128:(ot + 1) * 128,
                                      qg * QG + hf * 512:qg * QG + (hf + 1) * 512],
                             stg[j][:, :], sem=f"ato{j}", waits=(t_ev,))
                free_stg[j] = t_st
                last_store = t_st
            free_wo[wpb] = tmm
    sb.free(q_sb, *k_sb, *v_sb, acc, *prb, *prT, *ob_sb, *wob, *stg, bos, st)
    return last_store


def _atT_view(atT, QG, g, qt):
    v = atT[:, :].rearrange("p (nt c) -> p nt c", c=QG)
    return v[:, g * 8:(g + 1) * 8, qt * 128:(qt + 1) * 128]


def emit_l1ffn(ctx, xn_fm, w_nat, wT, bias_r, delta_in, delta_out, CH=512):
    b, sb = ctx.b, ctx.sb
    NC = TOK // CH
    wb = [sb.alloc(f"bf_w{i}", [128, D], F32) for i in range(2)]
    alpha = sb.alloc("bf_al", [128, NT], F32)
    sgn = sb.alloc("bf_sgn", [128, NT * NT * 128], BF16)
    bs = sb.alloc("bf_b", [128, NT], F32)
    t_bl = b.dma("scalar", bs[:, :], bias_r[:, :], sem="bfb")
    free_w = [None, None]
    t_alf = None
    for i in range(NT):
        pb = i % 2
        t_ld = b.dma("sync", wb[pb][:, :], w_nat[i * 128:(i + 1) * 128, :],
                     sem=f"bfw{pb}", waits=(free_w[pb],))
        t_r = b.dve("tensor_reduce", alpha[:, i:i + 1], wb[pb][:, :], AX,
                    ALU.add, apply_absolute_value=True, waits=(t_ld,))
        free_w[pb] = t_r
        t_alf = b.dve("tensor_scalar_mul", alpha[:, i:i + 1], alpha[:, i:i + 1],
                      1.0 / D, waits=(t_r,))
    t_sg = None
    for ot in range(NT):
        pb = ot % 2
        t_ld = b.dma("sync",
                     wb[pb][:, :].rearrange("p (na c) -> p na c", c=128),
                     wT[:, ot * 128:(ot + 1) * 128]
                     .rearrange("(na p) c -> p na c", p=128),
                     sem=f"bfw{pb}", waits=(free_w[pb],))
        blk = sgn[:, ot * NT * 128:(ot + 1) * NT * 128]
        t_s1 = b.dve("tensor_scalar", blk, wb[pb][:, :], 0.0, 2.0,
                     ALU.is_ge, ALU.mult, waits=(t_ld, t_sg))
        t_sg = b.dve("tensor_scalar_sub", blk, blk, 1.0, waits=(t_s1,))
        free_w[pb] = t_sg
    xnb = [sb.alloc(f"bf_x{i}", [128, NT * CH], BF16) for i in range(2)]
    dlb = [sb.alloc(f"bf_d{i}", [128, NT * CH], F32) for i in range(2)]
    free_xn = [None, None]
    free_dl = [None, None]
    free_ps = [None] * 6
    pfi = [0]
    t_st = None
    for c in range(NC):
        pb = c % 2
        xnt, dlt = xnb[pb], dlb[pb]
        t_xn = b.dma("sync", fm_sb_view(xnt, CH), fm_dram_view(xn_fm, c * CH, CH),
                     sem=f"bfx{pb}", waits=(free_xn[pb],))
        t_dli = b.dma("sync", fm_sb_view(dlt, CH), fm_dram_view(delta_in, c * CH, CH),
                      sem=f"bfd{pb}", waits=(free_dl[pb],))
        t_dl = None
        for ot in range(NT):
            i = pfi[0] % 6
            pfi[0] += 1
            ps = ctx.psf[i]
            tmm = None
            for col in range(CH // 512):
                for a in range(NT):
                    tmm = b.pe("matmul", ps[:, 0:512],
                               sgn[:, (ot * NT + a) * 128:(ot * NT + a + 1) * 128],
                               xnt[:, a * CH + col * 512:a * CH + (col + 1) * 512],
                               start=(a == 0), stop=(a == NT - 1),
                               waits=(t_xn, t_sg if c == 0 and ot == 0 else None,
                                      free_ps[i]) if (a == 0 and col == 0) else ())
            t_y = b.dve("tensor_scalar", _ffn_tmp(ctx, CH), ps[:, 0:CH],
                        alpha[:, ot:ot + 1], bs[:, ot:ot + 1], ALU.mult, ALU.add,
                        waits=(tmm, t_alf if c == 0 and ot == 0 else None,
                               t_bl if c == 0 and ot == 0 else None))
            t_dl = b.dve("tensor_add", dlt[:, ot * CH:(ot + 1) * CH],
                         dlt[:, ot * CH:(ot + 1) * CH], _ffn_tmp(ctx, CH),
                         waits=(t_y, t_dli if ot == 0 else None))
            free_ps[i] = t_dl
        free_xn[pb] = t_dl
        t_st = b.dma("sync", fm_dram_view(delta_out, c * CH, CH),
                     fm_sb_view(dlt, CH), sem=f"bfs{pb}", waits=(t_dl,))
        free_dl[pb] = t_st
    sb.free(*wb, alpha, sgn, bs, *xnb, *dlb, ctx._ffn_tmp_buf)
    ctx._ffn_tmp_buf = None
    return t_st


def _ffn_tmp(ctx, CH):
    if getattr(ctx, "_ffn_tmp_buf", None) is None:
        ctx._ffn_tmp_buf = ctx.sb.alloc("bf_tmp", [128, CH], F32)
    return ctx._ffn_tmp_buf[:, 0:CH]

import ml_dtypes
from concourse.bass_utils import run_bass_kernel_spmd

BF16NP = ml_dtypes.bfloat16
NCORES = 8


def _cont(a):
    return np.ascontiguousarray(a)


def _rearr(v, nt):
    return _cont(np.asarray(v).reshape(nt, 128).T)


def _run(build_fn, in_maps):
    ctx, outs = build_fn()
    nc = ctx.b.finalize()
    for m in in_maps:
        m["consts_f32"] = ctx.const_array()
        m.update(host_constants())
    res = run_bass_kernel_spmd(nc, in_maps, core_ids=list(range(NCORES)))
    return [{k: r[k] for k in outs} for r in res.results]


def _b_entry(tag):
    def f():
        ctx = Ctx()
        x = ctx.din("x_tm", [TOK, D], F32)
        xn = ctx.dout("xn_out", [D, TOK], BF16)
        t_id = ctx.load_constants()
        last = emit_entry(ctx, x, xn, tag, t_ident=t_id)
        ctx.b.phase_barrier()
        return ctx, ["xn_out"]
    return f


def _b_p1():
    ctx = Ctx()
    w = {}
    xn = ctx.din("xn_fm", [D, TOK], BF16)
    for nm, shp in [("WuT", [D, R]), ("WvT", [D, R]), ("Wg1T", [D, H]),
                    ("Wg2T", [H, R])]:
        w[nm] = ctx.din(nm, shp, F32)
    for nm, c in [("bu_r", 16), ("bv_r", 16), ("bg1_r", 4), ("bg2_r", 16)]:
        w[nm] = ctx.din(nm, [128, c], F32)
    A = ctx.dout("A_out", [R, TOK], F32)
    Bo = ctx.dout("B_out", [R, TOK], F32)
    sf = ctx.dout("sf_out", [128, 16], F32)
    ctx.load_constants()
    tA, tB, tsf = emit_l0pass1(ctx, xn, w, A, Bo, sf)
    ctx.b.phase_barrier()
    return ctx, ["A_out", "B_out", "sf_out"]


def _b_p2():
    ctx = Ctx()
    w = {}
    xn = ctx.din("xn_fm", [D, TOK], BF16)
    A = ctx.din("A_s", [R, TOK], F32)
    Bs = ctx.din("B_s", [R, TOK], F32)
    sfp = ctx.din("sf_partner", [128, 16], F32)
    mskd = ctx.din("mask", [128, 1], F32)
    w["WuT"] = ctx.din("WuT", [D, R], F32)
    w["convw_r"] = ctx.din("convw_r", [128, 48], F32)
    w["convb_r"] = ctx.din("convb_r", [128, 16], F32)
    w["xhalo_l"] = ctx.din("xhalo_l", [1, D], F32)
    w["xhalo_r"] = ctx.din("xhalo_r", [1, D], F32)
    delta = ctx.dout("delta_out", [D, TOK], F32)
    ctx.load_constants()
    last = emit_l0pass2(ctx, xn, A, Bs, sfp, mskd, w, delta)
    ctx.b.phase_barrier()
    return ctx, ["delta_out"]


def _b_quant():
    ctx = Ctx()
    w1T = ctx.din("w1T", [D, FF], F32)
    w2T = ctx.din("w2T", [FF, D], F32)
    w1q = ctx.dout("w1qT", [D, FF], BF16)
    w2q = ctx.dout("w2qT", [FF, D], BF16)
    ctx.load_constants()
    t1 = emit_quant(ctx, w1T, w1q, D, FF, "w1")
    ctx.b.phase_barrier()
    t2 = emit_quant(ctx, w2T, w2q, FF, D, "w2")
    ctx.b.phase_barrier()
    return ctx, ["w1qT", "w2qT"]


def _b_ffn0():
    ctx = Ctx()
    xn = ctx.din("xn_fm", [D, TOK], BF16)
    w1q = ctx.din("w1qT", [D, FF], BF16)
    w2q = ctx.din("w2qT", [FF, D], BF16)
    b1r = ctx.din("b1_r", [128, 64], F32)
    b2r = ctx.din("b2_r", [128, 16], F32)
    din = ctx.din("delta_in", [D, TOK], F32)
    dout = ctx.dout("delta_out", [D, TOK], F32)
    ctx.load_constants()
    last = emit_l0ffn(ctx, xn, w1q, w2q, b1r, b2r, din, dout)
    ctx.b.phase_barrier()
    return ctx, ["delta_out"]


def _b_exit(tag):
    def f():
        ctx = Ctx()
        x = ctx.din("x_tm", [TOK, D], F32)
        dl = ctx.din("delta_fm", [D, TOK], F32)
        o = ctx.dout("out_tm", [TOK, D], F32)
        ctx.load_constants()
        last = emit_exit(ctx, x, dl, o, tag)
        ctx.b.phase_barrier()
        return ctx, ["out_tm"]
    return f


def _b_proj(li):
    def f():
        ctx = Ctx()
        s = ctx.din("s_fm", [D, TOK], BF16)
        wq = ctx.din("wq", [D, D], BF16)
        wk = ctx.din("wk", [D, D], BF16)
        wv = ctx.din("wv", [D, D], BF16)
        bq = ctx.din("bq_r", [128, 16], BF16)
        bk = ctx.din("bk_r", [128, 16], BF16)
        bv = ctx.din("bv_row", [1, D], BF16)
        Q = ctx.dout("Qd", [D, TOK], BF16)
        K = ctx.dout("Kd", [D, TOK], BF16)
        V = ctx.dout("Vd", [TOK, D], BF16)
        ctx.load_constants()
        emit_l1proj(ctx, s, wq, wk, wv, bq, bk, bv, Q, K, V, 0)
        ctx.b.phase_barrier()
        return ctx, ["Qd", "Kd", "Vd"]
    return f


def _b_attn(li, out_f32):
    def f():
        ctx = Ctx()
        Q = ctx.din("Qd", [D, TOK], BF16)
        K = ctx.din("Kf", [2 * D, TOK], BF16)
        V = ctx.din("Vf", [KTOT, D], BF16)
        wo = ctx.din("wo", [D, D], BF16)
        bo = ctx.din("bo_r", [128, 16], BF16)
        o = ctx.dout("outd", [D, TOK], F32 if out_f32 else BF16)
        ctx.load_constants()
        last = emit_l1attn(ctx, Q, K, V, wo, bo, o, 0, out_f32=out_f32)
        ctx.b.wait("sync", last)
        ctx.b.phase_barrier()
        return ctx, ["outd"]
    return f


def _b_ffn1():
    ctx = Ctx()
    xn = ctx.din("xn_fm", [D, TOK], BF16)
    lw = ctx.din("lw", [D, D], F32)
    lwT = ctx.din("lwT", [D, D], F32)
    lb = ctx.din("lb_r", [128, 16], F32)
    din = ctx.din("delta_in", [D, TOK], F32)
    dout = ctx.dout("delta_out", [D, TOK], F32)
    ctx.load_constants()
    last = emit_l1ffn(ctx, xn, lw, lwT, lb, din, dout)
    ctx.b.phase_barrier()
    return ctx, ["delta_out"]


def kernel_unfused(**inputs):
    x_full = np.asarray(inputs["L0_attn_x"])
    WuT = _cont(np.asarray(inputs["L0_attn_Wu"]).T)
    WvT = _cont(np.asarray(inputs["L0_attn_Wv"]).T)
    Wg1T = _cont(np.asarray(inputs["L0_attn_Wg1"]).T)
    Wg2T = _cont(np.asarray(inputs["L0_attn_Wg2"]).T)
    bu_r = _rearr(inputs["L0_attn_bu"], 16)
    bv_r = _rearr(inputs["L0_attn_bv"], 16)
    bg1_r = _rearr(inputs["L0_attn_bg1"], 4)
    bg2_r = _rearr(inputs["L0_attn_bg2"], 16)
    convw = np.asarray(inputs["L0_attn_conv_w"]).reshape(D, 3)
    convw_r = _cont(convw.reshape(16, 128, 3).transpose(1, 0, 2).reshape(128, 48))
    convb_r = _rearr(inputs["L0_attn_conv_b"], 16)
    w1T = _cont(np.asarray(inputs["L0_ffn_w1"]).T)
    w2T = _cont(np.asarray(inputs["L0_ffn_w2"]).T)
    b1_r = _rearr(inputs["L0_ffn_b1"], 64)
    b2_r = _rearr(inputs["L0_ffn_b2"], 16)
    Wq = np.asarray(inputs["L1_attn_Wq"])
    Wk = np.asarray(inputs["L1_attn_Wk"])
    Wv1 = np.asarray(inputs["L1_attn_Wv"])
    Wo = np.asarray(inputs["L1_attn_Wo"])
    bq = np.asarray(inputs["L1_attn_bq"])
    bk = np.asarray(inputs["L1_attn_bk"])
    bv1 = np.asarray(inputs["L1_attn_bv"])
    bo = np.asarray(inputs["L1_attn_bo"])
    lw = np.asarray(inputs["L1_ffn_weight"])
    lwT = _cont(lw.T)
    lb_r = _rearr(inputs["L1_ffn_bias"], 16)

    def core_bh(c):
        return c // 2, c % 2

    xs = []
    for c in range(NCORES):
        b_, h = core_bh(c)
        xs.append(_cont(x_full[b_, h * TOK:(h + 1) * TOK, :]))

    xn0 = _run(_b_entry("e"), [dict(x_tm=xs[c]) for c in range(NCORES)])
    p1w = dict(WuT=WuT, WvT=WvT, Wg1T=Wg1T, Wg2T=Wg2T, bu_r=bu_r, bv_r=bv_r,
               bg1_r=bg1_r, bg2_r=bg2_r)
    p1 = _run(_b_p1, [dict(xn_fm=xn0[c]["xn_out"], **p1w) for c in range(NCORES)])
    zeros_row = np.zeros((1, D), np.float32)
    p2_maps = []
    for c in range(NCORES):
        b_, h = core_bh(c)
        hl = _cont(x_full[b_, h * TOK - 1:h * TOK, :]) if h == 1 else zeros_row
        hr = _cont(x_full[b_, TOK:TOK + 1, :]) if h == 0 else zeros_row
        p2_maps.append(dict(
            xn_fm=xn0[c]["xn_out"], A_s=p1[c]["A_out"], B_s=p1[c]["B_out"],
            sf_partner=p1[c ^ 1]["sf_out"],
            mask=np.full((128, 1), float(h), np.float32),
            WuT=WuT, convw_r=convw_r, convb_r=convb_r, xhalo_l=hl, xhalo_r=hr))
    p2 = _run(_b_p2, p2_maps)
    qt = _run(_b_quant, [dict(w1T=w1T, w2T=w2T) for _ in range(NCORES)])
    f0 = _run(_b_ffn0, [dict(xn_fm=xn0[c]["xn_out"], w1qT=qt[c]["w1qT"],
                             w2qT=qt[c]["w2qT"], b1_r=b1_r, b2_r=b2_r,
                             delta_in=p2[c]["delta_out"]) for c in range(NCORES)])
    x1 = _run(_b_exit("x"), [dict(x_tm=xs[c], delta_fm=f0[c]["delta_out"])
                              for c in range(NCORES)])
    xn1 = _run(_b_entry("e"), [dict(x_tm=x1[c]["out_tm"]) for c in range(NCORES)])
    s_cur = [xn1[c]["xn_out"] for c in range(NCORES)]
    delta1 = None
    for i in range(NL):
        pw = dict(wq=_cont(Wq[i]), wk=_cont(Wk[i]), wv=_cont(Wv1[i]),
                  bq_r=_cont(bq[i].reshape(16, 128).T.astype(BF16NP)),
                  bk_r=_cont(bk[i].reshape(16, 128).T.astype(BF16NP)),
                  bv_row=_cont(bv1[i].reshape(1, D)))
        pj = _run(_b_proj(i), [dict(s_fm=s_cur[c], **pw) for c in range(NCORES)])
        at_maps = []
        for c in range(NCORES):
            p = c ^ 1
            lo, hi = (c & ~1), (c | 1)
            Kf = _cont(np.concatenate([pj[lo]["Kd"], pj[hi]["Kd"]], axis=0))
            Vf = _cont(np.concatenate([pj[lo]["Vd"], pj[hi]["Vd"]], axis=0))
            at_maps.append(dict(Qd=pj[c]["Qd"], Kf=Kf, Vf=Vf, wo=_cont(Wo[i]),
                                bo_r=_cont(bo[i].reshape(16, 128).T.astype(BF16NP))))
        at = _run(_b_attn(i, out_f32=(i == NL - 1)), at_maps)
        if i == NL - 1:
            delta1 = [at[c]["outd"] for c in range(NCORES)]
        else:
            s_cur = [at[c]["outd"] for c in range(NCORES)]
    f1 = _run(_b_ffn1, [dict(xn_fm=xn1[c]["xn_out"], lw=lw, lwT=lwT, lb_r=lb_r,
                             delta_in=delta1[c]) for c in range(NCORES)])
    y = _run(_b_exit("x"), [dict(x_tm=x1[c]["out_tm"], delta_fm=f1[c]["delta_out"])
                             for c in range(NCORES)])
    out = np.empty((4, 8192, 2048), dtype=np.float32)
    for c in range(NCORES):
        b_, h = core_bh(c)
        out[b_, h * TOK:(h + 1) * TOK, :] = y[c]["out_tm"]
    return out


PAIRS = [[0, 1], [2, 3], [4, 5], [6, 7]]


def _b_fused():
    ctx = Ctx()
    b = ctx.b
    x = ctx.din("x_tm", [TOK, D], F32)
    w = {}
    for nm, shp in [("WuT", [D, R]), ("WvT", [D, R]), ("Wg1T", [D, H]),
                    ("Wg2T", [H, R]), ("w1T", [D, FF]), ("w2T", [FF, D])]:
        w[nm] = ctx.din(nm, shp, F32)
    for nm, c in [("bu_r", 16), ("bv_r", 16), ("bg1_r", 4), ("bg2_r", 16),
                  ("convw_r", 48), ("convb_r", 16), ("b1_r", 64), ("b2_r", 16)]:
        w[nm] = ctx.din(nm, [128, c], F32)
    w["xhalo_l"] = ctx.din("xhalo_l", [1, D], F32)
    w["xhalo_r"] = ctx.din("xhalo_r", [1, D], F32)
    mskd = ctx.din("mask", [128, 1], F32)
    wq = [ctx.din(f"wq{i}", [D, D], BF16) for i in range(NL)]
    wk = [ctx.din(f"wk{i}", [D, D], BF16) for i in range(NL)]
    wv = [ctx.din(f"wv{i}", [D, D], BF16) for i in range(NL)]
    wo = [ctx.din(f"wo{i}", [D, D], BF16) for i in range(NL)]
    bqr = [ctx.din(f"bq{i}_r", [128, 16], BF16) for i in range(NL)]
    bkr = [ctx.din(f"bk{i}_r", [128, 16], BF16) for i in range(NL)]
    bvr = [ctx.din(f"bv{i}_row", [1, D], BF16) for i in range(NL)]
    bor = [ctx.din(f"bo{i}_r", [128, 16], BF16) for i in range(NL)]
    lw = ctx.din("lw", [D, D], F32)
    lwT = ctx.din("lwT", [D, D], F32)
    lb = ctx.din("lb_r", [128, 16], F32)
    y = ctx.dout("y_tm", [TOK, D], F32)
    xn0 = ctx.dint("xn0", [D, TOK], BF16).ap()
    A_s = ctx.dint("A_s", [R, TOK], F32).ap()
    B_s = ctx.dint("B_s", [R, TOK], F32).ap()
    sf_loc = ctx.dint("sf_loc", [128, 16], F32)
    sf_all = ctx.dint("sf_all", [256, 16], F32)
    d0 = ctx.dint("d0", [D, TOK], F32).ap()
    d0b = ctx.dint("d0b", [D, TOK], F32).ap()
    w1q = ctx.dint("w1qT", [D, FF], BF16).ap()
    w2q = ctx.dint("w2qT", [FF, D], BF16).ap()
    x1 = ctx.dint("x1_tm", [TOK, D], F32).ap()
    xn1 = ctx.dint("xn1", [D, TOK], BF16).ap()
    s_d = [ctx.dint(f"s{i}", [D, TOK], BF16).ap() for i in range(NL - 1)]
    Qd = ctx.dint("Qd", [D, TOK], BF16).ap()
    Kl = ctx.dint("Kl", [D, TOK], BF16)
    Vl = ctx.dint("Vl", [TOK, D], BF16)
    Kf = ctx.dint("Kf", [2 * D, TOK], BF16)
    Vf = ctx.dint("Vf", [KTOT, D], BF16)
    d1 = ctx.dint("d1", [D, TOK], F32).ap()
    d1b = ctx.dint("d1b", [D, TOK], F32).ap()

    t_id = ctx.load_constants()
    emit_entry(ctx, x, xn0, "e0", t_ident=t_id)
    b.phase_barrier()
    emit_quant(ctx, w["w1T"], w1q, D, FF, "w1")
    b.phase_barrier()
    emit_quant(ctx, w["w2T"], w2q, FF, D, "w2")
    b.phase_barrier()
    _, _, t_sf = emit_l0pass1(ctx, xn0, w, A_s, B_s, sf_loc.ap())
    b.phase_barrier()
    t_cc = b.emit("gpsimd", "collective_compute", "AllGather", ALU.bypass,
                  replica_groups=PAIRS, ins=[sf_loc.ap().opt()],
                  outs=[sf_all.ap().opt()], inc="cc")
    b.wait("gpsimd", t_cc)
    b.phase_barrier()
    w["xhalo_l"], w["xhalo_r"] = w["xhalo_l"], w["xhalo_r"]
    emit_l0pass2(ctx, xn0, A_s, B_s, sf_all.ap()[0:128, :], mskd, w, d0)
    b.phase_barrier()
    emit_l0ffn(ctx, xn0, w1q, w2q, w["b1_r"], w["b2_r"], d0, d0b)
    b.phase_barrier()
    emit_exit(ctx, x, d0b, x1, "x0")
    b.phase_barrier()
    emit_entry(ctx, x1, xn1, "e1", t_ident=None)
    b.phase_barrier()
    s_cur = xn1
    for i in range(NL):
        emit_l1proj(ctx, s_cur, wq[i], wk[i], wv[i], bqr[i], bkr[i], bvr[i],
                    Qd, Kl.ap(), Vl.ap(), i)
        b.phase_barrier()
        b.emit("gpsimd", "collective_compute", "AllGather", ALU.bypass,
               replica_groups=PAIRS, ins=[Kl.ap().opt()],
               outs=[Kf.ap().opt()], inc="cc")
        t_cc = b.emit("gpsimd", "collective_compute", "AllGather", ALU.bypass,
                      replica_groups=PAIRS, ins=[Vl.ap().opt()],
                      outs=[Vf.ap().opt()], inc="cc")
        b.wait("gpsimd", t_cc)
        b.phase_barrier()
        last = (i == NL - 1)
        emit_l1attn(ctx, Qd, Kf.ap(), Vf.ap(), wo[i], bor[i],
                    d1 if last else s_d[i], i, out_f32=last)
        b.phase_barrier()
        if not last:
            s_cur = s_d[i]
    emit_l1ffn(ctx, xn1, lw, lwT, lb, d1, d1b)
    b.phase_barrier()
    emit_exit(ctx, x1, d1b, y, "x1")
    b.phase_barrier()
    return ctx, ["y_tm"]


def kernel_fused(**inputs):
    x_full = np.asarray(inputs["L0_attn_x"])
    WuT = _cont(np.asarray(inputs["L0_attn_Wu"]).T)
    WvT = _cont(np.asarray(inputs["L0_attn_Wv"]).T)
    Wg1T = _cont(np.asarray(inputs["L0_attn_Wg1"]).T)
    Wg2T = _cont(np.asarray(inputs["L0_attn_Wg2"]).T)
    convw = np.asarray(inputs["L0_attn_conv_w"]).reshape(D, 3)
    base = dict(
        WuT=WuT, WvT=WvT, Wg1T=Wg1T, Wg2T=Wg2T,
        bu_r=_rearr(inputs["L0_attn_bu"], 16), bv_r=_rearr(inputs["L0_attn_bv"], 16),
        bg1_r=_rearr(inputs["L0_attn_bg1"], 4), bg2_r=_rearr(inputs["L0_attn_bg2"], 16),
        convw_r=_cont(convw.reshape(16, 128, 3).transpose(1, 0, 2).reshape(128, 48)),
        convb_r=_rearr(inputs["L0_attn_conv_b"], 16),
        w1T=_cont(np.asarray(inputs["L0_ffn_w1"]).T),
        w2T=_cont(np.asarray(inputs["L0_ffn_w2"]).T),
        b1_r=_rearr(inputs["L0_ffn_b1"], 64), b2_r=_rearr(inputs["L0_ffn_b2"], 16),
        lw=np.asarray(inputs["L1_ffn_weight"]),
        lwT=_cont(np.asarray(inputs["L1_ffn_weight"]).T),
        lb_r=_rearr(inputs["L1_ffn_bias"], 16))
    for i in range(NL):
        base[f"wq{i}"] = _cont(np.asarray(inputs["L1_attn_Wq"])[i])
        base[f"wk{i}"] = _cont(np.asarray(inputs["L1_attn_Wk"])[i])
        base[f"wv{i}"] = _cont(np.asarray(inputs["L1_attn_Wv"])[i])
        base[f"wo{i}"] = _cont(np.asarray(inputs["L1_attn_Wo"])[i])
        base[f"bq{i}_r"] = _cont(np.asarray(inputs["L1_attn_bq"])[i].reshape(16, 128).T)
        base[f"bk{i}_r"] = _cont(np.asarray(inputs["L1_attn_bk"])[i].reshape(16, 128).T)
        base[f"bv{i}_row"] = _cont(np.asarray(inputs["L1_attn_bv"])[i].reshape(1, D))
        base[f"bo{i}_r"] = _cont(np.asarray(inputs["L1_attn_bo"])[i].reshape(16, 128).T)
    zeros_row = np.zeros((1, D), np.float32)
    in_maps = []
    for c in range(NCORES):
        b_, h = c // 2, c % 2
        m = dict(base)
        m["x_tm"] = _cont(x_full[b_, h * TOK:(h + 1) * TOK, :])
        m["xhalo_l"] = _cont(x_full[b_, h * TOK - 1:h * TOK, :]) if h == 1 else zeros_row
        m["xhalo_r"] = _cont(x_full[b_, TOK:TOK + 1, :]) if h == 0 else zeros_row
        m["mask"] = np.full((128, 1), float(h), np.float32)
        in_maps.append(m)
    ys = _run(_b_fused, in_maps)
    out = np.empty((4, 8192, 2048), dtype=np.float32)
    for c in range(NCORES):
        b_, h = c // 2, c % 2
        out[b_, h * TOK:(h + 1) * TOK, :] = ys[c]["y_tm"]
    return out


kernel = kernel_fused


# version 05 (final)
# speedup vs baseline: 1.0313x; 1.0313x over previous
import os
os.environ.setdefault("GAUGE_NTFF_IGNORE_DMA", "1")
import numpy as np
import concourse.bass as bass
import concourse.mybir as mybir

F32 = mybir.dt.float32
BF16 = mybir.dt.bfloat16
AF = mybir.ActivationFunctionType
ALU = mybir.AluOpType
AX = mybir.AxisListType.X

ENGINES = ("sync", "vector", "scalar", "tensor", "gpsimd")

D = 2048
NT = D // 128
TOK = 4096
R = 2048
H = 512
FF = 8192
NL = 4
KTOT = 8192
SCALE = float(1.0 / np.sqrt(2048.0))
EPS = 1e-5


class Builder:

    def __init__(self, nc):
        self.nc = nc
        self.ops = {e: [] for e in ENGINES}
        self.cnt = {}
        self.sems = {}

    def sem(self, name):
        if name not in self.sems:
            self.sems[name] = self.nc.alloc_semaphore(name)
            self.cnt[name] = 0
        return self.sems[name]

    def emit(self, eng, method, *args, waits=(), inc=None, inc_by=1, **kwargs):
        tok = None
        incs = []
        if inc is not None:
            self.sem(inc)
            self.cnt[inc] += inc_by
            tok = (inc, self.cnt[inc])
            incs = [(inc, inc_by)]
        ws = [w for w in waits if w is not None]
        for w in ws:
            self.sem(w[0])
        self.ops[eng].append((method, args, kwargs, ws, incs))
        return tok

    def dma(self, q, out, in_, sem=None, waits=(), **kw):
        return self.emit(q, "dma_start", waits=waits, inc=sem or ("q_" + q),
                         inc_by=16, out=out, in_=in_, **kw)

    def pe(self, method, *a, **kw):
        kw.setdefault("inc", "pe")
        return self.emit("tensor", method, *a, **kw)

    def dve(self, method, *a, **kw):
        kw.setdefault("inc", "dve")
        return self.emit("vector", method, *a, **kw)

    def act(self, method, *a, **kw):
        kw.setdefault("inc", "act")
        return self.emit("scalar", method, *a, **kw)

    def pool(self, method, *a, **kw):
        kw.setdefault("inc", "pool")
        return self.emit("gpsimd", method, *a, **kw)

    def wait(self, eng, *toks):
        for t in toks:
            if t is not None:
                self.sem(t[0])
                self.ops[eng].append(("wait_ge", (), {}, [t], []))

    def phase_barrier(self):
        snap = [(s, v) for s, v in self.cnt.items() if v > 0]
        for eng in ENGINES:
            for s, v in snap:
                self.wait(eng, (s, v))

    def barrier_token(self, eng):
        sem = {"tensor": "pe", "vector": "dve", "scalar": "act", "gpsimd": "pool",
               "sync": "q_sync_nop"}[eng]
        return self.emit(eng, "nop", inc=sem)

    def replay(self, eng_name, E):
        hi = {}
        for method, args, kwargs, waits, incs in self.ops[eng_name]:
            for s, v in waits:
                if hi.get(s, -1) < v:
                    E.wait_ge(self.sems[s], v)
                    hi[s] = v
            if method == "wait_ge":
                continue
            ins = getattr(E, method)(*args, **kwargs)
            for s, n in incs:
                ins.then_inc(self.sems[s], n)

    def finalize(self):
        nc = self.nc
        with nc.Block() as block:
            block.sync(lambda E: self.replay("sync", E))
            block.vector(lambda E: self.replay("vector", E))
            block.scalar(lambda E: self.replay("scalar", E))
            block.tensor(lambda E: self.replay("tensor", E))
            block.gpsimd(lambda E: self.replay("gpsimd", E))
        return nc


class SB:

    def __init__(self, nc, limit=None):
        self.nc = nc
        self.base = (nc.sbuf_base + 63) // 64 * 64
        self.limit = limit if limit is not None else (nc.sbuf_top - self.base - 1024)
        self.blocks = []
        self.n = 0

    def alloc(self, name, shape, dtype, align=64):
        assert len(shape) == 2 and shape[0] <= 128
        bpe = 2 if dtype == BF16 else 4
        size = shape[1] * bpe
        pos = 0
        for s, e, _ in sorted(self.blocks):
            if s - pos >= size:
                break
            pos = max(pos, e)
            pos = (pos + align - 1) // align * align
        assert pos + size <= self.limit, (
            f"SBUF overflow: {name} needs {size}B at {pos} (limit {self.limit})")
        self.n += 1
        h = self.nc.alloc_sbuf_tensor_at(f"{name}_{self.n}", shape, dtype,
                                         offset=self.base + pos)
        self.blocks.append((pos, pos + size, h))
        return h

    def free(self, *hs):
        self.blocks = [b for b in self.blocks if b[2] not in hs]

    def used(self):
        return max((e for _, e, _ in self.blocks), default=0)


class Ctx:

    def __init__(self, name="mk"):
        nc = bass.Bass("TRN2", target_bir_lowering=False)
        self.nc = nc
        self.b = Builder(nc)
        self.sb = SB(nc)
        self.psf = [nc.alloc_psum_tensor(f"psf{i}", [128, 512], F32) for i in range(6)]
        self.psb = [nc.alloc_psum_tensor(f"psb{i}", [128, 1024], BF16) for i in range(2)]
        self.ident = self.sb.alloc("ident_bf", [128, 128], BF16)
        self.identf = self.sb.alloc("ident_f", [128, 128], F32)
        self.NCONST = 16
        self.cpool = self.sb.alloc("cpool", [128, self.NCONST], F32)
        self.cvals = []
        self.dram = {}

    def cf(self, v):
        v = float(v)
        if v not in self.cvals:
            assert len(self.cvals) < self.NCONST
            self.cvals.append(v)
        i = self.cvals.index(v)
        return self.cpool[:, i:i + 1]

    def const_array(self):
        a = np.zeros((128, self.NCONST), dtype=np.float32)
        a[:, :len(self.cvals)] = np.array(self.cvals, dtype=np.float32)[None, :]
        return a

    def din(self, name, shape, dtype):
        h = self.nc.dram_tensor(name, list(shape), dtype, kind="ExternalInput")
        self.dram[name] = h
        return h

    def dout(self, name, shape, dtype):
        h = self.nc.dram_tensor(name, list(shape), dtype, kind="ExternalOutput")
        self.dram[name] = h
        return h

    def dint(self, name, shape, dtype):
        h = self.nc.dram_tensor(name, list(shape), dtype, kind="Internal")
        self.dram[name] = h
        return h

    def load_constants(self):
        ib = self.din("ident_in_bf", [128, 128], BF16)
        if_ = self.din("ident_in_f", [128, 128], F32)
        cin = self.din("consts_f32", [128, self.NCONST], F32)
        self.b.dma("sync", self.ident[:, :], ib[:, :], sem="init")
        self.b.dma("sync", self.identf[:, :], if_[:, :], sem="init")
        t2 = self.b.dma("sync", self.cpool[:, :], cin[:, :], sem="init")
        self.t_const = t2
        return t2


def host_constants():
    import ml_dtypes
    return {
        "ident_in_bf": np.eye(128, dtype=ml_dtypes.bfloat16),
        "ident_in_f": np.eye(128, dtype=np.float32),
    }


def emit_entry(ctx, x_tm, xn_out, tag, tok_n=TOK, t_ident=None):
    b, sb = ctx.b, ctx.sb
    xbuf = [sb.alloc(f"en_x{i}_{tag}", [128, D], F32) for i in range(2)]
    xnb = [sb.alloc(f"en_xn{i}_{tag}", [128, D], BF16) for i in range(2)]
    evb = [sb.alloc(f"en_ev{i}_{tag}", [128, D], BF16) for i in range(2)]
    st = sb.alloc(f"en_st_{tag}", [128, 8], F32)
    trash = sb.alloc(f"en_trash_{tag}", [128, D], F32)

    n_tiles = tok_n // 128
    free_x = [None, None]
    free_xs = [None, None]
    free_xn = [None, None]
    free_ev = [None, None]
    free_ps = [None, None]
    prev_var = None
    prev_rc = None
    last = None
    for it in range(n_tiles):
        pb = it % 2
        xt, xnt, evt = xbuf[pb], xnb[pb], evb[pb]
        t_ld = b.dma("sync", xt[:, :], x_tm[it * 128:(it + 1) * 128, :],
                     sem=f"en_x{pb}_{tag}", waits=(free_x[pb], free_xs[pb]))
        t_sq = b.act("activation", trash[:, :], xt[:, :], AF.Square,
                     accum_out=st[:, 1:2], waits=(t_ld, prev_var))
        free_xs[pb] = t_sq
        t_r0 = b.dve("tensor_reduce", st[:, 0:1], xt[:, :], AX, ALU.add, waits=(t_ld,))
        t_r1 = b.dve("tensor_scalar_mul", st[:, 0:1], st[:, 0:1], 1.0 / D,
                     waits=(t_r0,))
        t_r2 = b.dve("tensor_mul", st[:, 2:3], st[:, 0:1], st[:, 0:1], waits=(t_r1,))
        t_var = b.dve("scalar_tensor_tensor", st[:, 3:4], st[:, 1:2], 1.0 / D,
                      st[:, 2:3], ALU.mult, ALU.subtract, waits=(t_sq, t_r2))
        prev_var = t_var
        t_rs = b.act("activation", st[:, 4:5], st[:, 3:4], AF.Sqrt, bias=ctx.cf(EPS),
                     waits=(t_var, prev_rc, t_ident if it == 0 else None))
        t_rc = b.dve("reciprocal", st[:, 5:6], st[:, 4:5], waits=(t_rs,))
        prev_rc = t_rc
        t_xn = b.dve("tensor_scalar", xnt[:, :], xt[:, :], st[:, 0:1], st[:, 5:6],
                     ALU.subtract, ALU.mult, waits=(t_rc, free_xn[pb]))
        free_x[pb] = t_xn
        for half in range(2):
            pbank = ctx.psb[half]
            w0 = [t_xn, free_ps[half]] + ([t_ident] if (it == 0 and half == 0) else [])
            t_tr = None
            for j in range(8):
                dt = half * 8 + j
                t_tr = b.pe("transpose", pbank[:, j * 128:(j + 1) * 128],
                            xnt[:, dt * 128:(dt + 1) * 128], ctx.ident[:, :],
                            waits=tuple(w0))
                w0 = []
            t_ev = b.dve("tensor_copy", evt[:, half * 1024:(half + 1) * 1024],
                         pbank[:, :], waits=(t_tr, free_ev[pb] if half == 0 else None))
            free_ps[half] = t_ev
            if half == 1:
                free_xn[pb] = t_ev
        dst = xn_out[:, it * 128:(it + 1) * 128].rearrange("(nt p) c -> p nt c", p=128)
        src = evt[:, :].rearrange("p (nt c) -> p nt c", c=128)
        t_st = b.dma("sync", dst, src, sem=f"en_s{pb}_{tag}", waits=(free_ps[1],))
        free_ev[pb] = t_st
        last = t_st
    sb.free(*xbuf, *xnb, *evb, st, trash)
    return last


def wgrid_dst(wblk, na):
    return wblk[:, :].rearrange("p (na c) -> p na c", c=128)


def wgrid_src(w_dram, out0, width=128):
    return w_dram[:, out0:out0 + width].rearrange("(na p) c -> p na c", p=128)


def load_wblk(b, q, wblk, w_dram, ot, na, width=128, waits=(), sem=None,
              dst=None):
    if dst is None:
        dst = wblk[:, 0:na * width].rearrange("p (na c) -> p na c", c=width)
    return b.dma(q, dst, wgrid_src(w_dram, ot * width, width), waits=waits, sem=sem)


def fm_dram_view(t, c0, ch, nt=NT):
    return t[:, c0:c0 + ch].rearrange("(nt p) c -> p nt c", p=128)


def fm_sb_view(sb_t, ch, nt=NT):
    return sb_t[:, 0:nt * ch].rearrange("p (nt c) -> p nt c", c=ch)


def emit_l0pass1(ctx, xn_fm, w, A_out, B_out, sf_out, CH=256):
    b, sb = ctx.b, ctx.sb
    NC = TOK // CH
    NH = H // 128
    wu = sb.alloc("p1_wu", [128, NT * NT * 128], BF16)
    wg1 = sb.alloc("p1_wg1", [128, NT * NH * 128], BF16)
    wg2 = sb.alloc("p1_wg2", [128, NH * NT * 128], BF16)
    for ot in range(NT):
        b.dma("gpsimd",
              wu[:, ot * NT * 128:(ot + 1) * NT * 128].rearrange("p (na c) -> p na c", c=128),
              wgrid_src(w["WuT"], ot * 128), sem="p1w")
    for ot in range(NH):
        b.dma("gpsimd", wg1[:, ot * NT * 128:(ot + 1) * NT * 128].rearrange(
            "p (na c) -> p na c", c=128), wgrid_src(w["Wg1T"], ot * 128), sem="p1w")
    for ot in range(NT):
        t_wg = b.dma("gpsimd", wg2[:, ot * NH * 128:(ot + 1) * NH * 128].rearrange(
            "p (na c) -> p na c", c=128), wgrid_src(w["Wg2T"], ot * 128), sem="p1w")
    bu_s = sb.alloc("p1_bu", [128, NT], F32)
    bv_s = sb.alloc("p1_bv", [128, NT], F32)
    bg1_s = sb.alloc("p1_bg1", [128, NH], F32)
    bg2_s = sb.alloc("p1_bg2", [128, NT], F32)
    b.dma("sync", bu_s[:, :], w["bu_r"][:, :], sem="p1b")
    b.dma("sync", bv_s[:, :], w["bv_r"][:, :], sem="p1b")
    b.dma("sync", bg1_s[:, :], w["bg1_r"][:, :], sem="p1b")
    t_bias = b.dma("sync", bg2_s[:, :], w["bg2_r"][:, :], sem="p1b")
    t_wload = ("p1w", b.cnt["p1w"])
    wvb = [sb.alloc(f"p1_wv{i}", [128, NT * 128], BF16) for i in range(2)]
    xnb = [sb.alloc(f"p1_xn{i}", [128, NT * CH], BF16) for i in range(2)]
    hb = sb.alloc("p1_h", [128, NH * CH], BF16)
    gt = sb.alloc("p1_gt", [128, NT * CH], F32)
    ug = sb.alloc("p1_ug", [128, NT * CH], F32)
    vg = sb.alloc("p1_vg", [128, NT * CH], F32)
    kc = sb.alloc("p1_kc", [128, NT * CH], F32)
    carry = sb.alloc("p1_carry", [128, NT], F32)
    t_cz = b.dve("memset", carry[:, :], 0.0)

    free_xn = [None, None]
    free_wv = [None, None]
    free_ps = [None] * 6
    free_ug = None
    free_vg = None
    free_kc = None
    free_gt = None
    free_h = None
    t_scan_prev = t_cz
    pfi = [0]

    def nxps(waits):
        i = pfi[0] % 6
        pfi[0] += 1
        return i, ctx.psf[i], [free_ps[i]] + list(waits)

    t_A_st = t_B_st = None
    for c in range(NC):
        pb = c % 2
        xnt = xnb[pb]
        t_xn = b.dma("sync", fm_sb_view(xnt, CH), fm_dram_view(xn_fm, c * CH, CH),
                     sem=f"p1x{pb}", waits=(free_xn[pb],))
        t_h = None
        for ht in range(NH):
            i, ps, ws = nxps([t_xn, t_wload if c == 0 else None,
                              free_h if ht == 0 else None])
            tmm = None
            for a in range(NT):
                tmm = b.pe("matmul", ps[:, 0:CH],
                           wg1[:, (ht * NT + a) * 128:(ht * NT + a + 1) * 128],
                           xnt[:, a * CH:(a + 1) * CH],
                           start=(a == 0), stop=(a == NT - 1),
                           waits=tuple(ws) if a == 0 else ())
            t_h = b.act("activation", hb[:, ht * CH:(ht + 1) * CH], ps[:, 0:CH],
                        AF.Relu, bias=bg1_s[:, ht:ht + 1],
                        waits=(tmm, t_bias if c == 0 and ht == 0 else None))
            free_ps[i] = t_h
        t_g = None
        for rt in range(NT):
            i, ps, ws = nxps([t_h, free_gt if rt == 0 else None])
            tmm = None
            for a in range(NH):
                tmm = b.pe("matmul", ps[:, 0:CH],
                           wg2[:, (rt * NH + a) * 128:(rt * NH + a + 1) * 128],
                           hb[:, a * CH:(a + 1) * CH],
                           start=(a == 0), stop=(a == NH - 1),
                           waits=tuple(ws) if a == 0 else ())
            t_g = b.act("activation", gt[:, rt * CH:(rt + 1) * CH], ps[:, 0:CH],
                        AF.Sigmoid, bias=bg2_s[:, rt:rt + 1], waits=(tmm,))
            free_ps[i] = t_g
        free_h = t_g
        t_ug_last = None
        for rt in range(NT):
            i, ps, ws = nxps([t_g, free_ug if rt == 0 else None])
            tmm = None
            for a in range(NT):
                tmm = b.pe("matmul", ps[:, 0:CH],
                           wu[:, (rt * NT + a) * 128:(rt * NT + a + 1) * 128],
                           xnt[:, a * CH:(a + 1) * CH],
                           start=(a == 0), stop=(a == NT - 1),
                           waits=tuple(ws) if a == 0 else ())
            t_ug_last = b.dve("scalar_tensor_tensor", ug[:, rt * CH:(rt + 1) * CH],
                              ps[:, 0:CH], bu_s[:, rt:rt + 1], gt[:, rt * CH:(rt + 1) * CH],
                              ALU.add, ALU.mult, waits=(tmm,))
            free_ps[i] = t_ug_last
        t_vg_last = None
        for rt in range(NT):
            wb = wvb[rt % 2]
            t_wv = load_wblk(b, "gpsimd", wb, w["WvT"], rt, NT,
                             waits=(free_wv[rt % 2],), sem=f"p1wv{rt % 2}")
            i, ps, ws = nxps([t_wv, free_vg if rt == 0 else None])
            tmm = None
            for a in range(NT):
                tmm = b.pe("matmul", ps[:, 0:CH],
                           wb[:, a * 128:(a + 1) * 128],
                           xnt[:, a * CH:(a + 1) * CH],
                           start=(a == 0), stop=(a == NT - 1),
                           waits=tuple(ws) if a == 0 else ())
            free_wv[rt % 2] = tmm
            t_vg_last = b.dve("scalar_tensor_tensor", vg[:, rt * CH:(rt + 1) * CH],
                              ps[:, 0:CH], bv_s[:, rt:rt + 1], gt[:, rt * CH:(rt + 1) * CH],
                              ALU.add, ALU.mult, waits=(tmm,))
            free_ps[i] = t_vg_last
        free_gt = t_vg_last
        free_xn[pb] = t_vg_last
        t_scan = None
        for rt in range(NT):
            t_scan = b.dve("tensor_tensor_scan", kc[:, rt * CH:(rt + 1) * CH],
                           ug[:, rt * CH:(rt + 1) * CH], ug[:, rt * CH:(rt + 1) * CH],
                           carry[:, rt:rt + 1], ALU.add, ALU.bypass,
                           waits=(t_scan_prev if rt == 0 else None,
                                  free_kc if rt == 0 else None,
                                  t_ug_last if rt == 0 else None))
            b.dve("tensor_copy", carry[:, rt:rt + 1],
                  kc[:, rt * CH + CH - 1:rt * CH + CH], waits=(t_scan,))
        free_ug = t_scan
        t_scan_prev = ("dve", b.cnt["dve"])
        t_B_st = b.dma("sync", fm_dram_view(B_out, c * CH, CH), fm_sb_view(vg, CH),
                       sem=f"p1B{c % 2}", waits=(t_vg_last,))
        t_A = b.dve("tensor_mul", kc[:, 0:NT * CH], kc[:, 0:NT * CH], vg[:, 0:NT * CH],
                    waits=(t_scan_prev,))
        free_vg = t_B_st
        t_A_st = b.dma("sync", fm_dram_view(A_out, c * CH, CH), fm_sb_view(kc, CH),
                       sem=f"p1A{c % 2}", waits=(t_A,))
        free_kc = t_A_st
    t_sf = b.dma("sync", sf_out[:, :], carry[:, :], sem="p1sf", waits=(t_scan_prev,))
    sb.free(wu, wg1, wg2, bu_s, bv_s, bg1_s, bg2_s, *wvb, *xnb, hb, gt, ug, vg,
            kc, carry)
    return t_A_st, t_B_st, t_sf


def emit_l0pass2(ctx, xn_fm, A_s, B_s, sf_partner, mask, w, delta_out, CH=256):
    b, sb = ctx.b, ctx.sb
    NC = TOK // CH
    CH2 = CH + 2
    wu = sb.alloc("p2_wu", [128, NT * NT * 128], BF16)
    for ot in range(NT):
        b.dma("gpsimd",
              wu[:, ot * NT * 128:(ot + 1) * NT * 128].rearrange("p (na c) -> p na c", c=128),
              wgrid_src(w["WuT"], ot * 128), sem="p2w")
    t_wload = ("p2w", b.cnt["p2w"])
    prefix = sb.alloc("p2_prefix", [128, NT], F32)
    msk = sb.alloc("p2_msk", [128, 1], F32)
    cw = sb.alloc("p2_cw", [128, 3 * NT], F32)
    cb = sb.alloc("p2_cb", [128, NT], F32)
    b.dma("scalar", prefix[:, :], sf_partner[:, :], sem="p2c")
    b.dma("scalar", msk[:, :], mask[:, :], sem="p2c")
    b.dma("scalar", cw[:, :], w["convw_r"][:, :], sem="p2c")
    t_cbl = b.dma("scalar", cb[:, :], w["convb_r"][:, :], sem="p2c")
    t_pref = b.dve("tensor_scalar_mul", prefix[:, :], prefix[:, :], msk[:, 0:1],
                   waits=(t_cbl,))

    halo = sb.alloc("p2_halo", [128, 2 * NT], BF16)
    hx = sb.alloc("p2_hx", [1, D], F32)
    hxn = sb.alloc("p2_hxn", [1, D], BF16)
    hsq = sb.alloc("p2_hsq", [1, D], F32)
    hst = sb.alloc("p2_hst", [1, 8], F32)
    t_hev = None
    for side, name in ((0, "xhalo_l"), (1, "xhalo_r")):
        t_hl = b.dma("scalar", hx[:, :], w[name][:, :], sem="p2hx",
                     waits=(t_hev,))
        t_sq = b.act("activation", hsq[:, :], hx[:, :], AF.Square,
                     accum_out=hst[:, 1:2], waits=(t_hl, t_hev))
        t_r0 = b.dve("tensor_reduce", hst[:, 0:1], hx[:, :], AX, ALU.add,
                     waits=(t_hl,))
        t_r1 = b.dve("tensor_scalar_mul", hst[:, 0:1], hst[:, 0:1], 1.0 / D,
                     waits=(t_r0,))
        t_r2 = b.dve("tensor_mul", hst[:, 2:3], hst[:, 0:1], hst[:, 0:1],
                     waits=(t_r1,))
        t_var = b.dve("scalar_tensor_tensor", hst[:, 3:4], hst[:, 1:2], 1.0 / D,
                      hst[:, 2:3], ALU.mult, ALU.subtract, waits=(t_sq, t_r2))
        t_rs = b.act("activation", hst[:, 4:5], hst[:, 3:4], AF.Sqrt,
                     bias=ctx.cf(EPS)[0:1, :], waits=(t_var,))
        t_rc = b.dve("reciprocal", hst[:, 5:6], hst[:, 4:5], waits=(t_rs,))
        t_xn = b.dve("tensor_scalar", hxn[:, :], hx[:, :], hst[:, 0:1],
                     hst[:, 5:6], ALU.subtract, ALU.mult, waits=(t_rc,))
        ps = ctx.psf[5]
        tmm = None
        for dt in range(NT):
            tmm = b.pe("matmul", ps[:, dt:dt + 1],
                       hxn[:, dt * 128:(dt + 1) * 128], ctx.ident[0:1, 0:1],
                       start=True, stop=True,
                       waits=(t_xn, t_hev, ctx.t_const) if dt == 0 else ())
        t_hev = b.dve("tensor_copy", halo[:, side * NT:(side + 1) * NT],
                      ps[:, 0:NT], waits=(tmm,))

    Ab = sb.alloc("p2_A", [128, NT * CH], F32)
    Bb = sb.alloc("p2_B", [128, NT * CH], F32)
    Ab2 = sb.alloc("p2_A2", [128, NT * CH], F32)
    Bb2 = sb.alloc("p2_B2", [128, NT * CH], F32)
    xq = sb.alloc("p2_xq", [128, NT * CH2], BF16)
    xq2 = sb.alloc("p2_xq2", [128, NT * CH2], BF16)
    gl = sb.alloc("p2_gl", [128, NT * CH], BF16)
    cv = sb.alloc("p2_cv", [128, CH], F32)
    dl = sb.alloc("p2_dl", [128, NT * CH], F32)
    free_A = [None, None]
    free_B = [None, None]
    free_xq = [None, None]
    free_gl = None
    free_dl = None
    free_ps = [None] * 6
    free_ps[5] = t_hev
    pfi = [0]

    def nxps(waits):
        i = pfi[0] % 6
        pfi[0] += 1
        return i, ctx.psf[i], [free_ps[i]] + list(waits)

    t_d_st = None
    for c in range(NC):
        pb = c % 2
        At, Bt, xqt = (Ab, Bb, xq) if pb == 0 else (Ab2, Bb2, xq2)
        t_A = b.dma("sync", fm_sb_view(At, CH), fm_dram_view(A_s, c * CH, CH),
                    sem=f"p2A{pb}", waits=(free_A[pb],))
        t_B = b.dma("sync", fm_sb_view(Bt, CH), fm_dram_view(B_s, c * CH, CH),
                    sem=f"p2B{pb}", waits=(free_B[pb],))
        xqv = xqt[:, :].rearrange("p (nt c) -> p nt c", c=CH2)
        lo = 1 if c == 0 else 0
        hi = CH + 1 if c == NC - 1 else CH + 2
        t_x = b.dma("sync", xqv[:, :, lo:hi],
                    fm_dram_view(xn_fm, c * CH - 1 + lo, hi - lo),
                    sem=f"p2x{pb}", waits=(free_xq[pb],))
        t_xl = t_xr = None
        t_hcp = None
        if c == 0:
            for nt in range(NT):
                t_hcp = b.dve("tensor_copy", xqt[:, nt * CH2:nt * CH2 + 1],
                              halo[:, nt:nt + 1],
                              waits=(t_hev, t_x, free_xq[pb]) if nt == 0 else ())
        if c == NC - 1:
            for nt in range(NT):
                t_hcp = b.dve("tensor_copy",
                              xqt[:, nt * CH2 + CH + 1:nt * CH2 + CH + 2],
                              halo[:, NT + nt:NT + nt + 1],
                              waits=(t_hev, t_x, free_xq[pb]) if nt == 0 else ())
        if c == 0 and "dbg_xq" in w:
            tdbg = b.dma("gpsimd", w["dbg_xq"][:, :], xqt[:, 0:2 * CH2],
                         waits=(t_hcp, t_x), sem="dbg")
            b.dma("gpsimd", w["dbg_halo"][:, :], halo[:, :], waits=(t_hev,), sem="dbg")
        dbg_gl_pending = (c == 0 and "dbg_gl" in w)
        t_gl = None
        for rt in range(NT):
            t_gl = b.dve("scalar_tensor_tensor", gl[:, rt * CH:(rt + 1) * CH],
                         Bt[:, rt * CH:(rt + 1) * CH], prefix[:, rt:rt + 1],
                         At[:, rt * CH:(rt + 1) * CH], ALU.mult, ALU.add,
                         waits=(t_A, t_B, t_pref, free_gl) if rt == 0 else ())
        if c == 0 and "dbg_gl" in w:
            b.dma("gpsimd", w["dbg_gl"][:, :], gl[:, :], waits=(t_gl,), sem="dbg")
        free_A[pb] = t_gl
        free_B[pb] = t_gl
        t_dl = None
        for dt in range(NT):
            i, ps, ws = nxps([t_gl, t_wload if c == 0 and dt == 0 else None,
                              free_dl if dt == 0 else None])
            tmm = None
            for a in range(NT):
                tmm = b.pe("matmul", ps[:, 0:CH],
                           wu[:, (dt * NT + a) * 128:(dt * NT + a + 1) * 128],
                           gl[:, a * CH:(a + 1) * CH],
                           start=(a == 0), stop=(a == NT - 1),
                           waits=tuple(ws) if a == 0 else ())
            if dt == NT - 1:
                free_gl = tmm
            base = dt * CH2
            t_c0 = b.dve("tensor_scalar_mul", cv[:, 0:CH],
                         xqt[:, base:base + CH], cw[:, 3 * dt:3 * dt + 1],
                         waits=(t_x, t_xl, t_xr, t_hcp, t_cbl) if dt == 0 else ())
            t_c1 = b.dve("scalar_tensor_tensor", cv[:, 0:CH],
                         xqt[:, base + 1:base + 1 + CH], cw[:, 3 * dt + 1:3 * dt + 2],
                         cv[:, 0:CH], ALU.mult, ALU.add, waits=(t_c0,))
            t_c2 = b.dve("scalar_tensor_tensor", cv[:, 0:CH],
                         xqt[:, base + 2:base + 2 + CH], cw[:, 3 * dt + 2:3 * dt + 3],
                         cv[:, 0:CH], ALU.mult, ALU.add, waits=(t_c1,))
            t_dl = b.dve("scalar_tensor_tensor", dl[:, dt * CH:(dt + 1) * CH],
                         ps[:, 0:CH], cb[:, dt:dt + 1], cv[:, 0:CH],
                         ALU.add, ALU.add, waits=(tmm, t_c2, free_dl if dt == 0 else None))
            free_ps[i] = t_dl
            if dt == NT - 1:
                free_xq[pb] = t_dl
        t_d_st = b.dma("sync", fm_dram_view(delta_out, c * CH, CH),
                       fm_sb_view(dl, CH), sem=f"p2d{c % 2}", waits=(t_dl,))
        free_dl = t_d_st
    sb.free(wu, prefix, msk, cw, cb, halo, Ab, Bb, Ab2, Bb2, xq, xq2, gl, cv, dl,
            hx, hxn, hsq, hst)
    return t_d_st


def emit_quant(ctx, wT, wqT, IN, OUT, tag):
    b, sb = ctx.b, ctx.sb
    NB = IN // 128
    wb = [sb.alloc(f"q_w{i}_{tag}", [128, OUT], F32) for i in range(2)]
    acc = sb.alloc(f"q_acc_{tag}", [128, 4], F32)
    t1 = [sb.alloc(f"q_t1{i}_{tag}", [128, OUT], BF16) for i in range(2)]
    t2 = sb.alloc(f"q_t2_{tag}", [128, OUT], BF16)
    ones_row = sb.alloc(f"q_ones_{tag}", [1, 128], F32)
    tq = sb.alloc(f"q_tq_{tag}", [128, 2], F32)
    b.dve("memset", ones_row[:, :], 1.0)
    t_az = b.dve("memset", acc[:, :], 0.0)
    free_w = [None, None]
    t_red = None
    for i in range(NB):
        pb = i % 2
        t_ld = b.dma("sync", wb[pb][:, :], wT[i * 128:(i + 1) * 128, :],
                     sem=f"qw{pb}_{tag}", waits=(free_w[pb],))
        t_r = b.dve("tensor_reduce", acc[:, 1:2], wb[pb][:, :], AX, ALU.add,
                    apply_absolute_value=True, waits=(t_ld, t_az if i == 0 else None))
        t_red = b.dve("tensor_add", acc[:, 0:1], acc[:, 0:1], acc[:, 1:2],
                      waits=(t_r,))
        free_w[pb] = t_red
    ps = ctx.psf[0]
    t_mm = b.pe("matmul", ps[0:1, 0:1], acc[:, 0:1], ctx.cf(1.0),
                start=True, stop=True, waits=(t_red, ctx.t_const))
    t_sc = b.dve("tensor_scalar_mul", acc[0:1, 2:3], ps[0:1, 0:1],
                 0.7 / (IN * OUT), waits=(t_mm,))
    t_bc = b.pe("matmul", ps[:, 4:5], ones_row[:, :], acc[0:1, 2:3],
                start=True, stop=True, waits=(t_sc,))
    t_tq = b.dve("tensor_copy", tq[:, 0:1], ps[:, 4:5], waits=(t_bc,))
    t_tqn = b.dve("tensor_scalar_mul", tq[:, 1:2], tq[:, 0:1], -1.0,
                  waits=(t_tq,))
    t_st = None
    free_st = [None, None]
    for i in range(NB):
        pb = i % 2
        t_ld = b.dma("sync", wb[pb][:, :], wT[i * 128:(i + 1) * 128, :],
                     sem=f"qw{pb}_{tag}", waits=(free_w[pb],))
        t_m1 = b.dve("tensor_single_scalar", t1[pb][:, :], wb[pb][:, :], tq[:, 0:1],
                     ALU.is_ge, waits=(t_ld, t_tqn if i == 0 else None,
                                       free_st[pb]))
        t_m2 = b.dve("tensor_single_scalar", t2[:, :], wb[pb][:, :], tq[:, 1:2],
                     ALU.is_le, waits=(t_m1,))
        t_q = b.dve("tensor_sub", t1[pb][:, :], t1[pb][:, :], t2[:, :], waits=(t_m2,))
        free_w[pb] = t_q
        t_st = b.dma("sync", wqT[i * 128:(i + 1) * 128, :], t1[pb][:, :],
                     sem=f"qs{pb}_{tag}", waits=(t_q,))
        free_st[pb] = t_st
    sb.free(*wb, acc, *t1, t2, ones_row, tq)
    return t_st


def emit_l0ffn(ctx, xn_fm, w1qT, w2qT, b1_r, b2_r, delta_in, delta_out, CH=512):
    b, sb = ctx.b, ctx.sb
    NC = TOK // CH
    NF = FF // 128
    xnb = [sb.alloc(f"ff_xn{i}", [128, NT * CH], BF16) for i in range(2)]
    hb = sb.alloc("ff_h", [128, NF * CH], BF16)
    w1b = [sb.alloc(f"ff_w1{i}", [128, NT * 128], BF16) for i in range(2)]
    w2b = [sb.alloc(f"ff_w2{i}", [128, NF * 128], BF16) for i in range(2)]
    dlb = [sb.alloc(f"ff_dl{i}", [128, NT * CH], F32) for i in range(2)]
    b1s = sb.alloc("ff_b1", [128, NF], F32)
    b2s = sb.alloc("ff_b2", [128, NT], F32)
    b.dma("scalar", b1s[:, :], b1_r[:, :], sem="ffb")
    t_b = b.dma("scalar", b2s[:, :], b2_r[:, :], sem="ffb")
    free_xn = [None, None]
    free_w1 = [None, None]
    free_w2 = [None, None]
    free_dl = [None, None]
    free_ps = [None] * 6
    free_h = None
    pfi = [0]

    def nxps(waits):
        i = pfi[0] % 6
        pfi[0] += 1
        return i, ctx.psf[i], [free_ps[i]] + list(waits)

    t_st = None
    for c in range(NC):
        pb = c % 2
        xnt, dlt = xnb[pb], dlb[pb]
        t_xn = b.dma("sync", fm_sb_view(xnt, CH), fm_dram_view(xn_fm, c * CH, CH),
                     sem=f"ffx{pb}", waits=(free_xn[pb],))
        t_dli = b.dma("sync", fm_sb_view(dlt, CH), fm_dram_view(delta_in, c * CH, CH),
                      sem=f"ffd{pb}", waits=(free_dl[pb],))
        t_h = None
        for ft in range(NF):
            wpb = ft % 2
            t_w1 = load_wblk(b, "sync", w1b[wpb], w1qT, ft, NT,
                             waits=(free_w1[wpb],), sem=f"ffw1{wpb}")
            i, ps, ws = nxps([t_w1, t_xn, t_b if c == 0 and ft == 0 else None,
                              free_h if ft == 0 else None])
            ncol = CH // 512
            tmm = None
            for col in range(ncol):
                for a in range(NT):
                    tmm = b.pe("matmul", ps[:, col * 512:(col + 1) * 512],
                               w1b[wpb][:, a * 128:(a + 1) * 128],
                               xnt[:, a * CH + col * 512:a * CH + (col + 1) * 512],
                               start=(a == 0), stop=(a == NT - 1),
                               waits=tuple(ws) if (a == 0 and col == 0) else ())
            free_w1[wpb] = tmm
            t_h = b.act("activation", hb[:, ft * CH:(ft + 1) * CH], ps[:, 0:CH],
                        AF.Gelu, bias=b1s[:, ft:ft + 1], waits=(tmm,))
            free_ps[i] = t_h
        t_dl = None
        for dt in range(NT):
            wpb = dt % 2
            t_w2 = load_wblk(b, "sync", w2b[wpb], w2qT, dt, NF,
                             waits=(free_w2[wpb],), sem=f"ffw2{wpb}")
            i, ps, ws = nxps([t_w2, t_h])
            ncol = CH // 512
            tmm = None
            for col in range(ncol):
                for a in range(NF):
                    tmm = b.pe("matmul", ps[:, col * 512:(col + 1) * 512],
                               w2b[wpb][:, a * 128:(a + 1) * 128],
                               hb[:, a * CH + col * 512:a * CH + (col + 1) * 512],
                               start=(a == 0), stop=(a == NF - 1),
                               waits=tuple(ws) if (a == 0 and col == 0) else ())
            free_w2[wpb] = tmm
            if dt == NT - 1:
                free_h = tmm
            t_dl = b.dve("scalar_tensor_tensor", dlt[:, dt * CH:(dt + 1) * CH],
                         ps[:, 0:CH], b2s[:, dt:dt + 1], dlt[:, dt * CH:(dt + 1) * CH],
                         ALU.add, ALU.add, waits=(tmm, t_dli if dt == 0 else None))
            free_ps[i] = t_dl
        t_st = b.dma("sync", fm_dram_view(delta_out, c * CH, CH),
                     fm_sb_view(dlt, CH), sem=f"ffs{pb}", waits=(t_dl,))
        free_dl[pb] = t_st
        free_xn[pb] = t_dl
    sb.free(*xnb, hb, *w1b, *w2b, *dlb, b1s, b2s)
    return t_st


def emit_exit(ctx, x_tm, delta_fm, out_tm, tag, tok_n=TOK):
    b, sb = ctx.b, ctx.sb
    dlb = [sb.alloc(f"ex_dl{i}_{tag}", [128, D], F32) for i in range(2)]
    xb = [sb.alloc(f"ex_x{i}_{tag}", [128, D], F32) for i in range(2)]
    ob = [sb.alloc(f"ex_o{i}_{tag}", [128, D], F32) for i in range(2)]
    n_tiles = tok_n // 128
    free_dl = [None, None]
    free_x = [None, None]
    free_o = [None, None]
    free_ps = [None, None]
    last = None
    for it in range(n_tiles):
        pb = it % 2
        dlt, xt, ot = dlb[pb], xb[pb], ob[pb]
        t_dl = b.dma("sync", fm_sb_view(dlt, 128), fm_dram_view(delta_fm, it * 128, 128),
                     sem=f"exd{pb}_{tag}", waits=(free_dl[pb],))
        t_x = b.dma("sync", xt[:, :], x_tm[it * 128:(it + 1) * 128, :],
                    sem=f"exx{pb}_{tag}", waits=(free_x[pb],))
        t_add = None
        for g in range(4):
            ps = ctx.psf[g % 2]
            tmm = None
            for j in range(4):
                dt = g * 4 + j
                tmm = b.pe("transpose", ps[:, j * 128:(j + 1) * 128],
                           dlt[:, dt * 128:(dt + 1) * 128], ctx.identf[:, :],
                           waits=(t_dl, free_ps[g % 2], ctx.t_const if it == 0 and g == 0 else None)
                           if j == 0 else ())
            t_add = b.dve("tensor_add", ot[:, g * 512:(g + 1) * 512], ps[:, 0:512],
                          xt[:, g * 512:(g + 1) * 512],
                          waits=(tmm, t_x if g == 0 else None,
                                 free_o[pb] if g == 0 else None))
            free_ps[g % 2] = t_add
        free_dl[pb] = t_add
        free_x[pb] = t_add
        t_st = b.dma("sync", out_tm[it * 128:(it + 1) * 128, :], ot[:, :],
                     sem=f"exs{pb}_{tag}", waits=(t_add,))
        free_o[pb] = t_st
        last = t_st
    sb.free(*dlb, *xb, *ob)
    return last


def emit_l1ffn(ctx, xn_fm, w_nat, wT, bias_r, delta_in, delta_out, CH=512):
    b, sb = ctx.b, ctx.sb
    NC = TOK // CH
    wb = [sb.alloc(f"bf_w{i}", [128, D], F32) for i in range(2)]
    alpha = sb.alloc("bf_al", [128, NT], F32)
    sgn = sb.alloc("bf_sgn", [128, NT * NT * 128], BF16)
    bs = sb.alloc("bf_b", [128, NT], F32)
    t_bl = b.dma("scalar", bs[:, :], bias_r[:, :], sem="bfb")
    free_w = [None, None]
    t_alf = None
    for i in range(NT):
        pb = i % 2
        t_ld = b.dma("sync", wb[pb][:, :], w_nat[i * 128:(i + 1) * 128, :],
                     sem=f"bfw{pb}", waits=(free_w[pb],))
        t_r = b.dve("tensor_reduce", alpha[:, i:i + 1], wb[pb][:, :], AX,
                    ALU.add, apply_absolute_value=True, waits=(t_ld,))
        free_w[pb] = t_r
        t_alf = b.dve("tensor_scalar_mul", alpha[:, i:i + 1], alpha[:, i:i + 1],
                      1.0 / D, waits=(t_r,))
    t_sg = None
    for ot in range(NT):
        pb = ot % 2
        t_ld = b.dma("sync",
                     wb[pb][:, :].rearrange("p (na c) -> p na c", c=128),
                     wT[:, ot * 128:(ot + 1) * 128]
                     .rearrange("(na p) c -> p na c", p=128),
                     sem=f"bfw{pb}", waits=(free_w[pb],))
        blk = sgn[:, ot * NT * 128:(ot + 1) * NT * 128]
        t_s1 = b.dve("tensor_scalar", blk, wb[pb][:, :], 0.0, 2.0,
                     ALU.is_ge, ALU.mult, waits=(t_ld, t_sg))
        t_sg = b.dve("tensor_scalar_sub", blk, blk, 1.0, waits=(t_s1,))
        free_w[pb] = t_sg
    xnb = [sb.alloc(f"bf_x{i}", [128, NT * CH], BF16) for i in range(2)]
    dlb = [sb.alloc(f"bf_d{i}", [128, NT * CH], F32) for i in range(2)]
    free_xn = [None, None]
    free_dl = [None, None]
    free_ps = [None] * 6
    pfi = [0]
    t_st = None
    for c in range(NC):
        pb = c % 2
        xnt, dlt = xnb[pb], dlb[pb]
        t_xn = b.dma("sync", fm_sb_view(xnt, CH), fm_dram_view(xn_fm, c * CH, CH),
                     sem=f"bfx{pb}", waits=(free_xn[pb],))
        t_dli = b.dma("sync", fm_sb_view(dlt, CH), fm_dram_view(delta_in, c * CH, CH),
                      sem=f"bfd{pb}", waits=(free_dl[pb],))
        t_dl = None
        for ot in range(NT):
            i = pfi[0] % 6
            pfi[0] += 1
            ps = ctx.psf[i]
            tmm = None
            for col in range(CH // 512):
                for a in range(NT):
                    tmm = b.pe("matmul", ps[:, 0:512],
                               sgn[:, (ot * NT + a) * 128:(ot * NT + a + 1) * 128],
                               xnt[:, a * CH + col * 512:a * CH + (col + 1) * 512],
                               start=(a == 0), stop=(a == NT - 1),
                               waits=(t_xn, t_sg if c == 0 and ot == 0 else None,
                                      free_ps[i]) if (a == 0 and col == 0) else ())
            t_y = b.dve("tensor_scalar", _ffn_tmp(ctx, CH), ps[:, 0:CH],
                        alpha[:, ot:ot + 1], bs[:, ot:ot + 1], ALU.mult, ALU.add,
                        waits=(tmm, t_alf if c == 0 and ot == 0 else None,
                               t_bl if c == 0 and ot == 0 else None))
            t_dl = b.dve("tensor_add", dlt[:, ot * CH:(ot + 1) * CH],
                         dlt[:, ot * CH:(ot + 1) * CH], _ffn_tmp(ctx, CH),
                         waits=(t_y, t_dli if ot == 0 else None))
            free_ps[i] = t_dl
        free_xn[pb] = t_dl
        t_st = b.dma("sync", fm_dram_view(delta_out, c * CH, CH),
                     fm_sb_view(dlt, CH), sem=f"bfs{pb}", waits=(t_dl,))
        free_dl[pb] = t_st
    sb.free(*wb, alpha, sgn, bs, *xnb, *dlb, ctx._ffn_tmp_buf)
    ctx._ffn_tmp_buf = None
    return t_st


def _ffn_tmp(ctx, CH):
    if getattr(ctx, "_ffn_tmp_buf", None) is None:
        ctx._ffn_tmp_buf = ctx.sb.alloc("bf_tmp", [128, CH], F32)
    return ctx._ffn_tmp_buf[:, 0:CH]


def emit_l1proj(ctx, s_fm, wq, wk, wv, bq_r, bk_r, bv_row, Q_out, K_out, V_out,
                li, parts="all", s_state=None):
    b, sb = ctx.b, ctx.sb
    if s_state is None:
        wvb = sb.alloc(f"pj_wv_{li}", [128, NT * 1024], BF16)
        s_sb = sb.alloc(f"pj_s_{li}", [128, NT * TOK], BF16)
        t_s = b.dma("sync", s_sb[:, :].rearrange("p (nt c) -> p nt c", c=TOK),
                    fm_dram_view(s_fm, 0, TOK), sem="pjs")
        wb = [sb.alloc(f"pj_w{i}_{li}", [128, NT * 128], BF16) for i in range(2)]
        stg = [sb.alloc(f"pj_st{i}_{li}", [128, 512], BF16) for i in range(4)]
        bqs = sb.alloc(f"pj_bq_{li}", [128, NT], BF16)
        bks = sb.alloc(f"pj_bk_{li}", [128, NT], BF16)
        bvs = sb.alloc(f"pj_bv_{li}", [1, D], BF16)
        ones_bf = sb.alloc(f"pj_one_{li}", [1, 128], BF16)
        t_ob = b.dve("memset", ones_bf[:, :], 1.0)
        b.dma("scalar", bqs[:, :], bq_r[:, :], sem="pjb")
        b.dma("scalar", bks[:, :], bk_r[:, :], sem="pjb")
        t_b = b.dma("scalar", bvs[:, :], bv_row[:, :], sem="pjb")
        free_w = [None, None]
        free_stg = [None] * 4
        free_ps = [None] * 4
        s_state = dict(wvb=wvb, s_sb=s_sb, t_s=t_s, wb=wb, stg=stg, bqs=bqs,
                       bks=bks, bvs=bvs, ones_bf=ones_bf, t_ob=t_ob, t_b=t_b,
                       free_w=free_w, free_stg=free_stg, free_ps=free_ps)
    else:
        (wvb, s_sb, t_s, wb, stg, bqs, bks, bvs, ones_bf, t_ob, t_b, free_w,
         free_stg, free_ps) = (s_state[k] for k in (
            "wvb", "s_sb", "t_s", "wb", "stg", "bqs", "bks", "bvs", "ones_bf",
            "t_ob", "t_b", "free_w", "free_stg", "free_ps"))
    qk_list = {"all": ((wq, bqs, Q_out), (wk, bks, K_out)),
               "kv": ((wk, bks, K_out),),
               "q": ((wq, bqs, Q_out),)}[parts]
    for (wdr, bias, outd) in qk_list:
        for ot in range(NT):
            pb = ot % 2
            t_w = load_wblk(b, "sync", wb[pb], wdr, ot, NT,
                            waits=(free_w[pb],), sem=f"pjw{pb}")
            for tg in range(2):
                tmm = None
                for a in range(NT):
                    w0 = ([t_w, t_s] + [free_ps[j] for j in range(4)]
                          if a == 0 else [])
                    for j in range(4):
                        tc = tg * 4 + j
                        tmm = b.pe("matmul", ctx.psf[j][:, 0:512],
                                   wb[pb][:, a * 128:(a + 1) * 128],
                                   s_sb[:, a * TOK + tc * 512:a * TOK + (tc + 1) * 512],
                                   start=(a == 0), stop=(a == NT - 1),
                                   waits=tuple(w0) if j == 0 else ())
                        w0 = []
                if tg == 1:
                    free_w[pb] = tmm
                for j in range(4):
                    tc = tg * 4 + j
                    t_ev = b.act("activation", stg[j][:, :], ctx.psf[j][:, 0:512],
                                 AF.Identity, bias=bias[:, ot:ot + 1],
                                 waits=(tmm, free_stg[j],
                                        t_b if ot == 0 and tg == 0 else None))
                    free_ps[j] = t_ev
                    t_st = b.dma("sync",
                                 outd[ot * 128:(ot + 1) * 128, tc * 512:(tc + 1) * 512],
                                 stg[j][:, :], sem=f"pjo{j}", waits=(t_ev,))
                    free_stg[j] = t_st
    if parts == "q":
        sb.free(s_sb, *wb, wvb, *stg, bqs, bks, bvs, ones_bf)
        return None
    t_wv_free = None
    for vp in range(2):
        for dv2 in range(2):
            load_wblk(b, "sync", None, wv, vp * 2 + dv2, NT, width=512,
                      sem="pjwv", waits=(t_wv_free,) if dv2 == 0 else (),
                      dst=wvb[:, :].rearrange("p (na c) -> p na c", c=1024)
                      [:, :, dv2 * 512:(dv2 + 1) * 512])
        t_wv = ("pjwv", b.cnt["pjwv"])
        for tt in range(TOK // 128):
            tmm = None
            for a in range(NT):
                w0 = ([t_wv, t_s, t_ob, free_ps[0], free_ps[1]]
                      if a == 0 else [])
                for j in range(2):
                    tmm = b.pe("matmul", ctx.psf[j][:, 0:512],
                               s_sb[:, a * TOK + tt * 128:a * TOK + (tt + 1) * 128],
                               wvb[:, a * 1024 + j * 512:a * 1024 + (j + 1) * 512],
                               start=(a == 0), stop=False,
                               waits=tuple(w0) if j == 0 else ())
                    w0 = []
            for j in range(2):
                tmm = b.pe("matmul", ctx.psf[j][:, 0:512], ones_bf[:, :],
                           bvs[:, (vp * 2 + j) * 512:(vp * 2 + j + 1) * 512],
                           start=False, stop=True)
            for j in range(2):
                dvb = vp * 2 + j
                t_ev = b.dve("tensor_copy", stg[j][:, :], ctx.psf[j][:, 0:512],
                             waits=(tmm if j == 0 else None, free_stg[j]))
                free_ps[j] = t_ev
                t_st = b.dma("sync",
                             V_out[tt * 128:(tt + 1) * 128, dvb * 512:(dvb + 1) * 512],
                             stg[j][:, :], sem=f"pjo{j}", waits=(t_ev,))
                free_stg[j] = t_st
            if tt == TOK // 128 - 1:
                t_wv_free = tmm
    if parts == "kv":
        return s_state
    sb.free(s_sb, *wb, wvb, *stg, bqs, bks, bvs, ones_bf)
    return s_state


def emit_l1attn(ctx, Q_dram, K_dram, V_dram, wo, bo_r, out_dram, li,
                out_f32=False):
    b, sb = ctx.b, ctx.sb
    KS = 512
    NPR = KTOT // (2 * KS)
    QG = 1024
    NQG = TOK // QG
    NQT = QG // 128
    q_sb = sb.alloc(f"at_q_{li}", [128, NT * QG], BF16)
    atT = q_sb
    k_sb = [sb.alloc(f"at_k{i}_{li}", [128, NT * KS], BF16) for i in range(2)]
    v_sb = [sb.alloc(f"at_v{i}_{li}", [128, (KS // 128) * D], BF16) for i in range(2)]
    acc = sb.alloc(f"at_acc_{li}", [128, NQT * D], F32)
    prb = [sb.alloc(f"at_p{i}_{li}", [128, 2 * KS], BF16) for i in range(2)]
    prT = [sb.alloc(f"at_pt{i}_{li}", [128, 2 * KS], BF16) for i in range(2)]
    ob_sb = [sb.alloc(f"at_ob{i}_{li}", [128, D], BF16) for i in range(2)]
    wob = [sb.alloc(f"at_wo{i}_{li}", [128, NT * 128], BF16) for i in range(2)]
    stg = [sb.alloc(f"at_st{i}_{li}", [128, 512], BF16 if not out_f32 else F32)
           for i in range(2)]
    bos = sb.alloc(f"at_bo_{li}", [128, NT], BF16)
    st = sb.alloc(f"at_state_{li}", [128, 10 * NQT], F32)
    t_bo = b.dma("scalar", bos[:, :], bo_r[:, :], sem="atb")

    def S(qt, k):
        return st[:, qt * 10 + k:qt * 10 + k + 1]

    free_q = None
    free_k = [None, None]
    free_v = [None, None]
    free_sc = [None, None]
    free_pv = [None] * 4
    free_tb = None
    free_fb = None
    free_prb = [None, None]
    free_prT = [None, None]
    free_acc = [None] * NQT
    free_st10 = [None] * NQT
    free_wo = [None, None]
    free_stg = [None, None]
    free_ob = [None, None]
    last_store = None
    for qg in range(NQG):
        t_q = b.dma("sync", q_sb[:, :].rearrange("p (nt c) -> p nt c", c=QG),
                    fm_dram_view(Q_dram, qg * QG, QG, nt=NT),
                    sem=f"atq{qg % 2}", waits=(free_q,))
        t_z_arr = [None] * NQT
        for pr in range(NPR):
            half = pr // (NPR // 2)
            kcol = (pr % (NPR // 2)) * 2 * KS
            t_k = [None, None]
            t_v = [None, None]
            for s_ in range(2):
                t_k[s_] = b.dma(
                    "sync", k_sb[s_][:, :].rearrange("p (nt c) -> p nt c", c=KS),
                    K_dram[half * D:(half + 1) * D, kcol + s_ * KS:kcol + (s_ + 1) * KS]
                    .rearrange("(nt p) c -> p nt c", p=128),
                    sem=f"atk{s_}", waits=(free_k[s_],))
                t_v[s_] = b.dma(
                    "sync", v_sb[s_][:, :].rearrange("p (nk c) -> p nk c", c=D),
                    V_dram[half * (KTOT // 2) + kcol + s_ * KS:
                           half * (KTOT // 2) + kcol + (s_ + 1) * KS, :]
                    .rearrange("(nk p) c -> p nk c", p=128),
                    sem=f"atv{s_}", waits=(free_v[s_],))
            exp_tok = [None] * NQT
            pt_tok = [None] * NQT
            for qt in range(NQT + 1):
                if qt < NQT:
                    tmm = None
                    for a in range(NT):
                        w0 = ([t_k[0], t_k[1], t_q, free_sc[0], free_sc[1]]
                              if a == 0 else [])
                        for s_ in range(2):
                            tmm = b.pe("matmul", ctx.psf[s_][:, 0:KS],
                                       q_sb[:, a * QG + qt * 128:a * QG + (qt + 1) * 128],
                                       k_sb[s_][:, a * KS:(a + 1) * KS],
                                       start=(a == 0), stop=(a == NT - 1),
                                       waits=tuple(w0) if s_ == 0 else ())
                            w0 = []
                    if qt == NQT - 1:
                        free_k[0] = tmm
                        free_k[1] = tmm
                    first = (pr == 0)
                    t_r0 = b.dve("tensor_reduce", S(qt, 2), ctx.psf[0][:, 0:KS],
                                 AX, ALU.max,
                                 waits=(tmm, free_st10[qt] if first else None))
                    t_r1 = b.dve("tensor_reduce", S(qt, 9), ctx.psf[1][:, 0:KS],
                                 AX, ALU.max, waits=(t_r0,))
                    t_mx = b.dve("tensor_max", S(qt, 2), S(qt, 2), S(qt, 9),
                                 waits=(t_r1,))
                    if first:
                        t_mn = b.dve("tensor_copy", S(qt, 0), S(qt, 2), waits=(t_mx,))
                        t_al = None
                        t_bn = b.dve("tensor_scalar_mul", S(qt, 5), S(qt, 0),
                                     -SCALE, waits=(t_mn,))
                    else:
                        t_mn = b.dve("tensor_max", S(qt, 2), S(qt, 2), S(qt, 0),
                                     waits=(t_mx,))
                        t_dm = b.dve("tensor_sub", S(qt, 3), S(qt, 0), S(qt, 2),
                                     waits=(t_mn,))
                        t_dm2 = b.dve("tensor_scalar_mul", S(qt, 3), S(qt, 3),
                                      SCALE, waits=(t_dm,))
                        t_cp = b.dve("tensor_copy", S(qt, 0), S(qt, 2), waits=(t_dm2,))
                        t_al = b.act("activation", S(qt, 4), S(qt, 3), AF.Exp,
                                     waits=(t_dm2,))
                        t_bn = b.dve("tensor_scalar_mul", S(qt, 5), S(qt, 0),
                                     -SCALE, waits=(t_cp,))
                    pp = qt % 2
                    t_e0 = b.act("activation", prb[pp][:, 0:KS], ctx.psf[0][:, 0:KS],
                                 AF.Exp, bias=S(qt, 5), scale=ctx.cf(SCALE),
                                 accum_out=S(qt, 6),
                                 waits=(t_bn, free_prb[pp],
                                        ctx.t_const if qg == 0 and pr == 0 and qt == 0 else None))
                    t_e1 = b.act("activation", prb[pp][:, KS:2 * KS], ctx.psf[1][:, 0:KS],
                                 AF.Exp, bias=S(qt, 5), scale=ctx.cf(SCALE),
                                 accum_out=S(qt, 7), waits=(t_e0,))
                    free_sc[0] = t_e0
                    free_sc[1] = t_e1
                    exp_tok[qt] = (t_e0, t_e1)
                    if first:
                        t_z = b.dve("tensor_add", S(qt, 1), S(qt, 6), S(qt, 7),
                                    waits=(t_e1,))
                    else:
                        t_z0 = b.dve("scalar_tensor_tensor", S(qt, 1), S(qt, 1),
                                     S(qt, 4), S(qt, 6), ALU.mult, ALU.add,
                                     waits=(t_e1, t_al))
                        t_z = b.dve("tensor_add", S(qt, 1), S(qt, 1), S(qt, 7),
                                    waits=(t_z0,))
                    t_z_arr[qt] = t_z
                if qt >= 1:
                    sq = qt - 1
                    pp = sq % 2
                    first = (pr == 0)
                    ttr = None
                    for j in range(8):
                        ttr = b.pe("transpose", ctx.psb[0][:, j * 128:(j + 1) * 128],
                                   prb[pp][:, j * 128:(j + 1) * 128], ctx.ident[:, :],
                                   waits=(exp_tok[sq][0], exp_tok[sq][1], free_tb)
                                   if j == 0 else ())
                    free_prb[pp] = ttr
                    t_pt = b.dve("tensor_copy", prT[pp][:, :],
                                 ctx.psb[0][:, 0:2 * KS], waits=(ttr, free_prT[pp]))
                    free_tb = t_pt
                    pv_last = None
                    for j in range(8):
                        w0 = ([t_pt, t_v[0], t_v[1]] + [free_pv[d] for d in range(4)]
                              if j == 0 else [])
                        for dvb in range(4):
                            pv_last = b.pe(
                                "matmul", ctx.psf[2 + dvb][:, 0:512],
                                prT[pp][:, j * 128:(j + 1) * 128],
                                v_sb[j // 4][:, (j % 4) * D + dvb * 512:
                                             (j % 4) * D + (dvb + 1) * 512],
                                start=(j == 0), stop=(j == 7),
                                waits=tuple(w0) if dvb == 0 else ())
                            w0 = []
                    free_prT[pp] = pv_last
                    if qt == NQT:
                        free_v[0] = pv_last
                        free_v[1] = pv_last
                    for dvb in range(4):
                        a_sl = acc[:, sq * D + dvb * 512:sq * D + (dvb + 1) * 512]
                        if first:
                            t_ac = b.dve("tensor_copy", a_sl,
                                         ctx.psf[2 + dvb][:, 0:512],
                                         waits=(pv_last,
                                                free_acc[sq] if dvb == 0 else None))
                        else:
                            t_ac = b.dve("scalar_tensor_tensor", a_sl, a_sl,
                                         S(sq, 4), ctx.psf[2 + dvb][:, 0:512],
                                         ALU.mult, ALU.add, waits=(pv_last,))
                        free_pv[dvb] = t_ac
        t_attnT = None
        for qt in range(NQT):
            opb = qt % 2
            t_rz = b.dve("reciprocal", S(qt, 8), S(qt, 1), waits=(t_z_arr[qt],))
            t_obv = b.dve("tensor_scalar_mul", ob_sb[opb][:, :],
                          acc[:, qt * D:(qt + 1) * D], S(qt, 8),
                          waits=(t_rz, free_ob[opb]))
            free_acc[qt] = t_obv
            free_st10[qt] = t_obv
            for g in range(2):
                ttr = None
                for j in range(8):
                    dv = g * 8 + j
                    ttr = b.pe("transpose", ctx.psb[1][:, j * 128:(j + 1) * 128],
                               ob_sb[opb][:, dv * 128:(dv + 1) * 128],
                               ctx.ident[:, :],
                               waits=(t_obv, free_fb) if j == 0 else ())
                t_ev = b.dve("tensor_copy", _atT_view(atT, QG, g, qt),
                             ctx.psb[1][:, 0:1024].rearrange(
                                 "p (j c) -> p j c", c=128), waits=(ttr,))
                free_fb = t_ev
            t_attnT = free_fb
            free_ob[opb] = ttr
        for ot in range(NT):
            wpb = ot % 2
            t_w = load_wblk(b, "sync", wob[wpb], wo, ot, NT,
                            waits=(free_wo[wpb],), sem=f"atw{wpb}")
            tmm = None
            for a in range(NT):
                w0 = ([t_w, t_attnT, free_sc[0], free_sc[1]] if a == 0 else [])
                for hf in range(2):
                    tmm = b.pe("matmul", ctx.psf[hf][:, 0:512],
                               wob[wpb][:, a * 128:(a + 1) * 128],
                               atT[:, a * QG + hf * 512:a * QG + (hf + 1) * 512],
                               start=(a == 0), stop=(a == NT - 1),
                               waits=tuple(w0) if hf == 0 else ())
                    w0 = []
            free_wo[wpb] = tmm
            if ot == NT - 1:
                free_q = tmm
            for hf in range(2):
                t_ev = b.act("activation", stg[hf][:, :], ctx.psf[hf][:, 0:512],
                             AF.Identity, bias=bos[:, ot:ot + 1],
                             waits=(tmm, free_stg[hf],
                                    t_bo if qg == 0 and ot == 0 and hf == 0 else None))
                free_sc[hf] = t_ev
                t_st = b.dma("sync",
                             out_dram[ot * 128:(ot + 1) * 128,
                                      qg * QG + hf * 512:qg * QG + (hf + 1) * 512],
                             stg[hf][:, :], sem=f"ato{hf}", waits=(t_ev,))
                free_stg[hf] = t_st
                last_store = t_st
    sb.free(q_sb, *k_sb, *v_sb, acc, *prb, *prT, *ob_sb, *wob, *stg, bos, st)
    return last_store


def _atT_view(atT, QG, g, qt):
    v = atT[:, :].rearrange("p (nt c) -> p nt c", c=QG)
    return v[:, g * 8:(g + 1) * 8, qt * 128:(qt + 1) * 128]

import ml_dtypes
from concourse.bass_utils import run_bass_kernel_spmd

BF16NP = ml_dtypes.bfloat16
NCORES = 8


def _cont(a):
    return np.ascontiguousarray(a)


def _rearr(v, nt):
    return _cont(np.asarray(v).reshape(nt, 128).T)


def _run(build_fn, in_maps):
    ctx, outs = build_fn()
    nc = ctx.b.finalize()
    for m in in_maps:
        m["consts_f32"] = ctx.const_array()
        m.update(host_constants())
    res = run_bass_kernel_spmd(nc, in_maps, core_ids=list(range(NCORES)))
    return [{k: r[k] for k in outs} for r in res.results]


def _b_entry(tag):
    def f():
        ctx = Ctx()
        x = ctx.din("x_tm", [TOK, D], F32)
        xn = ctx.dout("xn_out", [D, TOK], BF16)
        t_id = ctx.load_constants()
        last = emit_entry(ctx, x, xn, tag, t_ident=t_id)
        ctx.b.phase_barrier()
        return ctx, ["xn_out"]
    return f


def _b_p1():
    ctx = Ctx()
    w = {}
    xn = ctx.din("xn_fm", [D, TOK], BF16)
    for nm, shp in [("WuT", [D, R]), ("WvT", [D, R]), ("Wg1T", [D, H]),
                    ("Wg2T", [H, R])]:
        w[nm] = ctx.din(nm, shp, F32)
    for nm, c in [("bu_r", 16), ("bv_r", 16), ("bg1_r", 4), ("bg2_r", 16)]:
        w[nm] = ctx.din(nm, [128, c], F32)
    A = ctx.dout("A_out", [R, TOK], F32)
    Bo = ctx.dout("B_out", [R, TOK], F32)
    sf = ctx.dout("sf_out", [128, 16], F32)
    ctx.load_constants()
    tA, tB, tsf = emit_l0pass1(ctx, xn, w, A, Bo, sf)
    ctx.b.phase_barrier()
    return ctx, ["A_out", "B_out", "sf_out"]


def _b_p2():
    ctx = Ctx()
    w = {}
    xn = ctx.din("xn_fm", [D, TOK], BF16)
    A = ctx.din("A_s", [R, TOK], F32)
    Bs = ctx.din("B_s", [R, TOK], F32)
    sfp = ctx.din("sf_partner", [128, 16], F32)
    mskd = ctx.din("mask", [128, 1], F32)
    w["WuT"] = ctx.din("WuT", [D, R], F32)
    w["convw_r"] = ctx.din("convw_r", [128, 48], F32)
    w["convb_r"] = ctx.din("convb_r", [128, 16], F32)
    w["xhalo_l"] = ctx.din("xhalo_l", [1, D], F32)
    w["xhalo_r"] = ctx.din("xhalo_r", [1, D], F32)
    delta = ctx.dout("delta_out", [D, TOK], F32)
    ctx.load_constants()
    last = emit_l0pass2(ctx, xn, A, Bs, sfp, mskd, w, delta)
    ctx.b.phase_barrier()
    return ctx, ["delta_out"]


def _b_quant():
    ctx = Ctx()
    w1T = ctx.din("w1T", [D, FF], F32)
    w2T = ctx.din("w2T", [FF, D], F32)
    w1q = ctx.dout("w1qT", [D, FF], BF16)
    w2q = ctx.dout("w2qT", [FF, D], BF16)
    ctx.load_constants()
    t1 = emit_quant(ctx, w1T, w1q, D, FF, "w1")
    ctx.b.phase_barrier()
    t2 = emit_quant(ctx, w2T, w2q, FF, D, "w2")
    ctx.b.phase_barrier()
    return ctx, ["w1qT", "w2qT"]


def _b_ffn0():
    ctx = Ctx()
    xn = ctx.din("xn_fm", [D, TOK], BF16)
    w1q = ctx.din("w1qT", [D, FF], BF16)
    w2q = ctx.din("w2qT", [FF, D], BF16)
    b1r = ctx.din("b1_r", [128, 64], F32)
    b2r = ctx.din("b2_r", [128, 16], F32)
    din = ctx.din("delta_in", [D, TOK], F32)
    dout = ctx.dout("delta_out", [D, TOK], F32)
    ctx.load_constants()
    last = emit_l0ffn(ctx, xn, w1q, w2q, b1r, b2r, din, dout)
    ctx.b.phase_barrier()
    return ctx, ["delta_out"]


def _b_exit(tag):
    def f():
        ctx = Ctx()
        x = ctx.din("x_tm", [TOK, D], F32)
        dl = ctx.din("delta_fm", [D, TOK], F32)
        o = ctx.dout("out_tm", [TOK, D], F32)
        ctx.load_constants()
        last = emit_exit(ctx, x, dl, o, tag)
        ctx.b.phase_barrier()
        return ctx, ["out_tm"]
    return f


def _b_proj(li):
    def f():
        ctx = Ctx()
        s = ctx.din("s_fm", [D, TOK], BF16)
        wq = ctx.din("wq", [D, D], BF16)
        wk = ctx.din("wk", [D, D], BF16)
        wv = ctx.din("wv", [D, D], BF16)
        bq = ctx.din("bq_r", [128, 16], BF16)
        bk = ctx.din("bk_r", [128, 16], BF16)
        bv = ctx.din("bv_row", [1, D], BF16)
        Q = ctx.dout("Qd", [D, TOK], BF16)
        K = ctx.dout("Kd", [D, TOK], BF16)
        V = ctx.dout("Vd", [TOK, D], BF16)
        ctx.load_constants()
        emit_l1proj(ctx, s, wq, wk, wv, bq, bk, bv, Q, K, V, 0)
        ctx.b.phase_barrier()
        return ctx, ["Qd", "Kd", "Vd"]
    return f


def _b_attn(li, out_f32):
    def f():
        ctx = Ctx()
        Q = ctx.din("Qd", [D, TOK], BF16)
        K = ctx.din("Kf", [2 * D, TOK], BF16)
        V = ctx.din("Vf", [KTOT, D], BF16)
        wo = ctx.din("wo", [D, D], BF16)
        bo = ctx.din("bo_r", [128, 16], BF16)
        o = ctx.dout("outd", [D, TOK], F32 if out_f32 else BF16)
        ctx.load_constants()
        last = emit_l1attn(ctx, Q, K, V, wo, bo, o, 0, out_f32=out_f32)
        ctx.b.wait("sync", last)
        ctx.b.phase_barrier()
        return ctx, ["outd"]
    return f


def _b_ffn1():
    ctx = Ctx()
    xn = ctx.din("xn_fm", [D, TOK], BF16)
    lw = ctx.din("lw", [D, D], F32)
    lwT = ctx.din("lwT", [D, D], F32)
    lb = ctx.din("lb_r", [128, 16], F32)
    din = ctx.din("delta_in", [D, TOK], F32)
    dout = ctx.dout("delta_out", [D, TOK], F32)
    ctx.load_constants()
    last = emit_l1ffn(ctx, xn, lw, lwT, lb, din, dout)
    ctx.b.phase_barrier()
    return ctx, ["delta_out"]


def kernel_unfused(**inputs):
    x_full = np.asarray(inputs["L0_attn_x"])
    WuT = _cont(np.asarray(inputs["L0_attn_Wu"]).T)
    WvT = _cont(np.asarray(inputs["L0_attn_Wv"]).T)
    Wg1T = _cont(np.asarray(inputs["L0_attn_Wg1"]).T)
    Wg2T = _cont(np.asarray(inputs["L0_attn_Wg2"]).T)
    bu_r = _rearr(inputs["L0_attn_bu"], 16)
    bv_r = _rearr(inputs["L0_attn_bv"], 16)
    bg1_r = _rearr(inputs["L0_attn_bg1"], 4)
    bg2_r = _rearr(inputs["L0_attn_bg2"], 16)
    convw = np.asarray(inputs["L0_attn_conv_w"]).reshape(D, 3)
    convw_r = _cont(convw.reshape(16, 128, 3).transpose(1, 0, 2).reshape(128, 48))
    convb_r = _rearr(inputs["L0_attn_conv_b"], 16)
    w1T = _cont(np.asarray(inputs["L0_ffn_w1"]).T)
    w2T = _cont(np.asarray(inputs["L0_ffn_w2"]).T)
    b1_r = _rearr(inputs["L0_ffn_b1"], 64)
    b2_r = _rearr(inputs["L0_ffn_b2"], 16)
    Wq = np.asarray(inputs["L1_attn_Wq"])
    Wk = np.asarray(inputs["L1_attn_Wk"])
    Wv1 = np.asarray(inputs["L1_attn_Wv"])
    Wo = np.asarray(inputs["L1_attn_Wo"])
    bq = np.asarray(inputs["L1_attn_bq"])
    bk = np.asarray(inputs["L1_attn_bk"])
    bv1 = np.asarray(inputs["L1_attn_bv"])
    bo = np.asarray(inputs["L1_attn_bo"])
    lw = np.asarray(inputs["L1_ffn_weight"])
    lwT = _cont(lw.T)
    lb_r = _rearr(inputs["L1_ffn_bias"], 16)

    def core_bh(c):
        return c // 2, c % 2

    xs = []
    for c in range(NCORES):
        b_, h = core_bh(c)
        xs.append(_cont(x_full[b_, h * TOK:(h + 1) * TOK, :]))

    xn0 = _run(_b_entry("e"), [dict(x_tm=xs[c]) for c in range(NCORES)])
    p1w = dict(WuT=WuT, WvT=WvT, Wg1T=Wg1T, Wg2T=Wg2T, bu_r=bu_r, bv_r=bv_r,
               bg1_r=bg1_r, bg2_r=bg2_r)
    p1 = _run(_b_p1, [dict(xn_fm=xn0[c]["xn_out"], **p1w) for c in range(NCORES)])
    zeros_row = np.zeros((1, D), np.float32)
    p2_maps = []
    for c in range(NCORES):
        b_, h = core_bh(c)
        hl = _cont(x_full[b_, h * TOK - 1:h * TOK, :]) if h == 1 else zeros_row
        hr = _cont(x_full[b_, TOK:TOK + 1, :]) if h == 0 else zeros_row
        p2_maps.append(dict(
            xn_fm=xn0[c]["xn_out"], A_s=p1[c]["A_out"], B_s=p1[c]["B_out"],
            sf_partner=p1[c ^ 1]["sf_out"],
            mask=np.full((128, 1), float(h), np.float32),
            WuT=WuT, convw_r=convw_r, convb_r=convb_r, xhalo_l=hl, xhalo_r=hr))
    p2 = _run(_b_p2, p2_maps)
    qt = _run(_b_quant, [dict(w1T=w1T, w2T=w2T) for _ in range(NCORES)])
    f0 = _run(_b_ffn0, [dict(xn_fm=xn0[c]["xn_out"], w1qT=qt[c]["w1qT"],
                             w2qT=qt[c]["w2qT"], b1_r=b1_r, b2_r=b2_r,
                             delta_in=p2[c]["delta_out"]) for c in range(NCORES)])
    x1 = _run(_b_exit("x"), [dict(x_tm=xs[c], delta_fm=f0[c]["delta_out"])
                              for c in range(NCORES)])
    xn1 = _run(_b_entry("e"), [dict(x_tm=x1[c]["out_tm"]) for c in range(NCORES)])
    s_cur = [xn1[c]["xn_out"] for c in range(NCORES)]
    delta1 = None
    for i in range(NL):
        pw = dict(wq=_cont(Wq[i]), wk=_cont(Wk[i]), wv=_cont(Wv1[i]),
                  bq_r=_cont(bq[i].reshape(16, 128).T.astype(BF16NP)),
                  bk_r=_cont(bk[i].reshape(16, 128).T.astype(BF16NP)),
                  bv_row=_cont(bv1[i].reshape(1, D)))
        pj = _run(_b_proj(i), [dict(s_fm=s_cur[c], **pw) for c in range(NCORES)])
        at_maps = []
        for c in range(NCORES):
            p = c ^ 1
            lo, hi = (c & ~1), (c | 1)
            Kf = _cont(np.concatenate([pj[lo]["Kd"], pj[hi]["Kd"]], axis=0))
            Vf = _cont(np.concatenate([pj[lo]["Vd"], pj[hi]["Vd"]], axis=0))
            at_maps.append(dict(Qd=pj[c]["Qd"], Kf=Kf, Vf=Vf, wo=_cont(Wo[i]),
                                bo_r=_cont(bo[i].reshape(16, 128).T.astype(BF16NP))))
        at = _run(_b_attn(i, out_f32=(i == NL - 1)), at_maps)
        if i == NL - 1:
            delta1 = [at[c]["outd"] for c in range(NCORES)]
        else:
            s_cur = [at[c]["outd"] for c in range(NCORES)]
    f1 = _run(_b_ffn1, [dict(xn_fm=xn1[c]["xn_out"], lw=lw, lwT=lwT, lb_r=lb_r,
                             delta_in=delta1[c]) for c in range(NCORES)])
    y = _run(_b_exit("x"), [dict(x_tm=x1[c]["out_tm"], delta_fm=f1[c]["delta_out"])
                             for c in range(NCORES)])
    out = np.empty((4, 8192, 2048), dtype=np.float32)
    for c in range(NCORES):
        b_, h = core_bh(c)
        out[b_, h * TOK:(h + 1) * TOK, :] = y[c]["out_tm"]
    return out


PAIRS = [[0, 1], [2, 3], [4, 5], [6, 7]]


def _b_fused():
    ctx = Ctx()
    b = ctx.b
    x = ctx.din("x_tm", [TOK, D], F32)
    w = {}
    for nm, shp in [("WuT", [D, R]), ("WvT", [D, R]), ("Wg1T", [D, H]),
                    ("Wg2T", [H, R]), ("w1T", [D, FF]), ("w2T", [FF, D])]:
        w[nm] = ctx.din(nm, shp, F32)
    for nm, c in [("bu_r", 16), ("bv_r", 16), ("bg1_r", 4), ("bg2_r", 16),
                  ("convw_r", 48), ("convb_r", 16), ("b1_r", 64), ("b2_r", 16)]:
        w[nm] = ctx.din(nm, [128, c], F32)
    w["xhalo_l"] = ctx.din("xhalo_l", [1, D], F32)
    w["xhalo_r"] = ctx.din("xhalo_r", [1, D], F32)
    mskd = ctx.din("mask", [128, 1], F32)
    wq = [ctx.din(f"wq{i}", [D, D], BF16) for i in range(NL)]
    wk = [ctx.din(f"wk{i}", [D, D], BF16) for i in range(NL)]
    wv = [ctx.din(f"wv{i}", [D, D], BF16) for i in range(NL)]
    wo = [ctx.din(f"wo{i}", [D, D], BF16) for i in range(NL)]
    bqr = [ctx.din(f"bq{i}_r", [128, 16], BF16) for i in range(NL)]
    bkr = [ctx.din(f"bk{i}_r", [128, 16], BF16) for i in range(NL)]
    bvr = [ctx.din(f"bv{i}_row", [1, D], BF16) for i in range(NL)]
    bor = [ctx.din(f"bo{i}_r", [128, 16], BF16) for i in range(NL)]
    lw = ctx.din("lw", [D, D], F32)
    lwT = ctx.din("lwT", [D, D], F32)
    lb = ctx.din("lb_r", [128, 16], F32)
    y = ctx.dout("y_tm", [TOK, D], F32)
    xn0 = ctx.dint("xn0", [D, TOK], BF16).ap()
    A_s = ctx.dint("A_s", [R, TOK], F32).ap()
    B_s = ctx.dint("B_s", [R, TOK], F32).ap()
    sf_loc = ctx.dint("sf_loc", [128, 16], F32)
    sf_all = ctx.dint("sf_all", [256, 16], F32)
    d0 = ctx.dint("d0", [D, TOK], F32).ap()
    d0b = ctx.dint("d0b", [D, TOK], F32).ap()
    w1q = ctx.dint("w1qT", [D, FF], BF16).ap()
    w2q = ctx.dint("w2qT", [FF, D], BF16).ap()
    x1 = ctx.dint("x1_tm", [TOK, D], F32).ap()
    xn1 = ctx.dint("xn1", [D, TOK], BF16).ap()
    s_d = [ctx.dint(f"s{i}", [D, TOK], BF16).ap() for i in range(NL - 1)]
    Qd = ctx.dint("Qd", [D, TOK], BF16).ap()
    Kl = ctx.dint("Kl", [D, TOK], BF16)
    Vl = ctx.dint("Vl", [TOK, D], BF16)
    Kf = ctx.dint("Kf", [2 * D, TOK], BF16)
    Vf = ctx.dint("Vf", [KTOT, D], BF16)
    d1 = ctx.dint("d1", [D, TOK], F32).ap()
    d1b = ctx.dint("d1b", [D, TOK], F32).ap()

    t_id = ctx.load_constants()
    emit_entry(ctx, x, xn0, "e0", t_ident=t_id)
    b.phase_barrier()
    emit_quant(ctx, w["w1T"], w1q, D, FF, "w1")
    b.phase_barrier()
    emit_quant(ctx, w["w2T"], w2q, FF, D, "w2")
    b.phase_barrier()
    _, _, t_sf = emit_l0pass1(ctx, xn0, w, A_s, B_s, sf_loc.ap())
    b.phase_barrier()
    t_cc = b.emit("gpsimd", "collective_compute", "AllGather", ALU.bypass,
                  replica_groups=PAIRS, ins=[sf_loc.ap().opt()],
                  outs=[sf_all.ap().opt()], inc="cc")
    b.wait("gpsimd", t_cc)
    b.phase_barrier()
    w["xhalo_l"], w["xhalo_r"] = w["xhalo_l"], w["xhalo_r"]
    emit_l0pass2(ctx, xn0, A_s, B_s, sf_all.ap()[0:128, :], mskd, w, d0)
    b.phase_barrier()
    emit_l0ffn(ctx, xn0, w1q, w2q, w["b1_r"], w["b2_r"], d0, d0b)
    b.phase_barrier()
    emit_exit(ctx, x, d0b, x1, "x0")
    b.phase_barrier()
    emit_entry(ctx, x1, xn1, "e1", t_ident=None)
    b.phase_barrier()
    s_cur = xn1
    for i in range(NL):
        st_ = emit_l1proj(ctx, s_cur, wq[i], wk[i], wv[i], bqr[i], bkr[i],
                          bvr[i], Qd, Kl.ap(), Vl.ap(), i, parts="kv")
        for j in range(4):
            b.wait("gpsimd", (f"pjo{j}", b.cnt.get(f"pjo{j}", 0)))
        b.emit("gpsimd", "collective_compute", "AllGather", ALU.bypass,
               replica_groups=PAIRS, ins=[Kl.ap().opt()],
               outs=[Kf.ap().opt()], inc="cc")
        t_cc = b.emit("gpsimd", "collective_compute", "AllGather", ALU.bypass,
                      replica_groups=PAIRS, ins=[Vl.ap().opt()],
                      outs=[Vf.ap().opt()], inc="cc")
        emit_l1proj(ctx, s_cur, wq[i], wk[i], wv[i], bqr[i], bkr[i], bvr[i],
                    Qd, Kl.ap(), Vl.ap(), i, parts="q", s_state=st_)
        b.wait("gpsimd", t_cc)
        b.phase_barrier()
        last = (i == NL - 1)
        emit_l1attn(ctx, Qd, Kf.ap(), Vf.ap(), wo[i], bor[i],
                    d1 if last else s_d[i], i, out_f32=last)
        b.phase_barrier()
        if not last:
            s_cur = s_d[i]
    emit_l1ffn(ctx, xn1, lw, lwT, lb, d1, d1b)
    b.phase_barrier()
    emit_exit(ctx, x1, d1b, y, "x1")
    b.phase_barrier()
    return ctx, ["y_tm"]


def kernel_fused(**inputs):
    x_full = np.asarray(inputs["L0_attn_x"])
    WuT = _cont(np.asarray(inputs["L0_attn_Wu"]).T)
    WvT = _cont(np.asarray(inputs["L0_attn_Wv"]).T)
    Wg1T = _cont(np.asarray(inputs["L0_attn_Wg1"]).T)
    Wg2T = _cont(np.asarray(inputs["L0_attn_Wg2"]).T)
    convw = np.asarray(inputs["L0_attn_conv_w"]).reshape(D, 3)
    base = dict(
        WuT=WuT, WvT=WvT, Wg1T=Wg1T, Wg2T=Wg2T,
        bu_r=_rearr(inputs["L0_attn_bu"], 16), bv_r=_rearr(inputs["L0_attn_bv"], 16),
        bg1_r=_rearr(inputs["L0_attn_bg1"], 4), bg2_r=_rearr(inputs["L0_attn_bg2"], 16),
        convw_r=_cont(convw.reshape(16, 128, 3).transpose(1, 0, 2).reshape(128, 48)),
        convb_r=_rearr(inputs["L0_attn_conv_b"], 16),
        w1T=_cont(np.asarray(inputs["L0_ffn_w1"]).T),
        w2T=_cont(np.asarray(inputs["L0_ffn_w2"]).T),
        b1_r=_rearr(inputs["L0_ffn_b1"], 64), b2_r=_rearr(inputs["L0_ffn_b2"], 16),
        lw=np.asarray(inputs["L1_ffn_weight"]),
        lwT=_cont(np.asarray(inputs["L1_ffn_weight"]).T),
        lb_r=_rearr(inputs["L1_ffn_bias"], 16))
    for i in range(NL):
        base[f"wq{i}"] = _cont(np.asarray(inputs["L1_attn_Wq"])[i])
        base[f"wk{i}"] = _cont(np.asarray(inputs["L1_attn_Wk"])[i])
        base[f"wv{i}"] = _cont(np.asarray(inputs["L1_attn_Wv"])[i])
        base[f"wo{i}"] = _cont(np.asarray(inputs["L1_attn_Wo"])[i])
        base[f"bq{i}_r"] = _cont(np.asarray(inputs["L1_attn_bq"])[i].reshape(16, 128).T)
        base[f"bk{i}_r"] = _cont(np.asarray(inputs["L1_attn_bk"])[i].reshape(16, 128).T)
        base[f"bv{i}_row"] = _cont(np.asarray(inputs["L1_attn_bv"])[i].reshape(1, D))
        base[f"bo{i}_r"] = _cont(np.asarray(inputs["L1_attn_bo"])[i].reshape(16, 128).T)
    zeros_row = np.zeros((1, D), np.float32)
    in_maps = []
    for c in range(NCORES):
        b_, h = c // 2, c % 2
        m = dict(base)
        m["x_tm"] = _cont(x_full[b_, h * TOK:(h + 1) * TOK, :])
        m["xhalo_l"] = _cont(x_full[b_, h * TOK - 1:h * TOK, :]) if h == 1 else zeros_row
        m["xhalo_r"] = _cont(x_full[b_, TOK:TOK + 1, :]) if h == 0 else zeros_row
        m["mask"] = np.full((128, 1), float(h), np.float32)
        in_maps.append(m)
    ys = _run(_b_fused, in_maps)
    out = np.empty((4, 8192, 2048), dtype=np.float32)
    for c in range(NCORES):
        b_, h = c // 2, c % 2
        out[b_, h * TOK:(h + 1) * TOK, :] = ys[c]["y_tm"]
    return out


kernel = kernel_fused
